# version 03
# baseline (speedup 1.0000x reference)
import contextlib
import numpy as np
import ml_dtypes

import concourse.bass as bass
import concourse.mybir as mybir
from concourse.bass_utils import run_bass_kernel_spmd

F32 = mybir.dt.float32
BF16 = mybir.dt.bfloat16
I32 = mybir.dt.int32
FP8 = mybir.dt.float8e4
F16 = mybir.dt.float16
AF = mybir.ActivationFunctionType
ALU = mybir.AluOpType
AX = mybir.AxisListType

P = 128
NT = 16
L = 2048
D = 1024
HD = 4096
EPS = 1e-5
ENGINES = ("sync", "gpsimd", "tensor", "vector", "scalar")
RSTD, NMU, RM, MCL, RC, SX, DSC, SXW, DSC2, NMUP = range(10)


def build(taps=(), n_phases=9):
    nc = bass.Bass("TRN2", target_bir_lowering=False)

    def din(name, shape, dt=F32):
        return nc.dram_tensor(name, list(shape), dt, kind="ExternalInput").ap()

    X = din("x", (L, D))
    WQKVT = din("wqkvT", (D, 3 * D))
    BQKVT = din("bqkvT", (P, 24))
    WOT = din("woT", (D, D))
    BOROW = din("borow", (1, D))
    BVROW = din("bvrow", (1, D))
    LN1GB = din("ln1gb", (2, D))
    LN2GB = din("ln2gb", (2, HD))
    B1ROW = din("b1row", (1, HD))
    B2ROW = din("b2row", (1, D))
    W1T = din("w1T", (D, HD))
    W2T = din("w2T", (HD, D))
    WQKV1T = din("wqkv1T", (D, 192))
    WOUT1T = din("wout1T", (64, D))
    WGT = din("wgT", (D, HD))
    WUT = din("wuT", (D, HD))
    WDT = din("wdT", (HD, D))
    IDF = din("identf", (P, P))
    IDB = din("identb", (P, P), BF16)
    CMASK = din("cmask", (P, P), BF16)

    Y = nc.dram_tensor("y", [L, D], F32, kind="ExternalOutput").ap()
    dbg = {}
    rawtap = {}
    for t in taps:
        if t.startswith("raw:"):
            _, nm, r, c, dt = t.split(":")
            rawtap[nm] = nc.dram_tensor("dbg_" + nm, [int(r), int(c)],
                                        {"bf16": BF16, "f32": F32}[dt],
                                        kind="ExternalOutput").ap()
        else:
            dbg[t] = nc.dram_tensor("dbg_" + t, [L, D], F32, kind="ExternalOutput").ap()

    C = {"qS": 0, "qG": 0, "tS": 0, "vS": 0, "sS": 0}
    sems = {}
    cur = {k: [] for k in ENGINES}

    def q(eng, fn, inc=None, waits=()):
        cur[eng].append((tuple(waits), fn, inc))

    def mk_runner(lst):
        def run(e):
            for waits, fn, inc in lst:
                for (s, v) in waits:
                    e.wait_ge(sems[s], v)
                if fn is not None:
                    inst = fn(e)
                    if inc is not None:
                        inst.then_inc(sems[inc[0]], inc[1])
        return run

    def close_phase():
        for eng in ENGINES:
            if cur[eng]:
                getattr(block, eng)(mk_runner(cur[eng]))
                cur[eng] = []

    def T(fn, waits=()):
        C["tS"] += 1
        q("tensor", fn, ("tS", 1), waits)
        return ("tS", C["tS"])

    def Tn(fn, waits=()):
        q("tensor", fn, None, waits)

    def V(fn, waits=()):
        C["vS"] += 1
        q("vector", fn, ("vS", 1), waits)
        return ("vS", C["vS"])

    def Vn(fn, waits=()):
        q("vector", fn, None, waits)

    def S(fn, waits=()):
        C["sS"] += 1
        q("scalar", fn, ("sS", 1), waits)
        return ("sS", C["sS"])

    def Sn(fn, waits=()):
        q("scalar", fn, None, waits)

    def DS(dst, src, waits=()):
        C["qS"] += 16
        q("sync", (lambda e, d=dst, s=src: e.dma_start(out=d, in_=s)), ("qS", 16), waits)
        return ("qS", C["qS"])

    def DG(dst, src, waits=()):
        C["qG"] += 16
        q("gpsimd", (lambda e, d=dst, s=src: e.dma_start(out=d, in_=s)), ("qG", 16), waits)
        return ("qG", C["qG"])

    def barrier():
        toks = [(k, C[k]) for k in C if C[k] > 0]
        for eng in ENGINES:
            q(eng, None, None, toks)

    es = contextlib.ExitStack()
    with es:
        for sn in ("qS", "qG", "tS", "vS", "sS"):
            sems[sn] = es.enter_context(nc.semaphore(sn))
        block = es.enter_context(nc.Block())

        stream = es.enter_context(nc.sbuf_tensor("stream", [P, NT * D], F32))
        identf = es.enter_context(nc.sbuf_tensor("sb_identf", [P, P], F32))
        identb = es.enter_context(nc.sbuf_tensor("sb_identb", [P, P], BF16))
        cmask = es.enter_context(nc.sbuf_tensor("sb_cmask", [P, P], BF16))
        bqkvT = es.enter_context(nc.sbuf_tensor("sb_bqkvT", [P, 24], F32))
        ones_f = es.enter_context(nc.sbuf_tensor("ones_f", [1, P], F32))
        ones_b = es.enter_context(nc.sbuf_tensor("ones_b", [1, P], BF16))
        colb = es.enter_context(nc.sbuf_tensor("colb", [P, 16], F32))
        aggrb = es.enter_context(nc.sbuf_tensor("aggrb", [P, 2], F32))
        bnb = es.enter_context(nc.sbuf_tensor("bnb", [P, 48], F32))
        mx8 = es.enter_context(nc.sbuf_tensor("mx8", [P, 8], F32))
        srcT = es.enter_context(nc.sbuf_tensor("srcT", [P, 2], F32))
        rowA = es.enter_context(nc.sbuf_tensor("rowA", [1, P], F16))
        rowB = es.enter_context(nc.sbuf_tensor("rowB", [2, P], F16))
        rowAug = es.enter_context(nc.sbuf_tensor("rowAug", [1, P], F16))
        scr1 = es.enter_context(nc.sbuf_tensor("scr1", [P, D], F32))
        sc2 = es.enter_context(nc.sbuf_tensor("sc2", [1, 2], F32))
        epscol = es.enter_context(nc.sbuf_tensor("epscol", [P, 1], F32))
        statlog = (es.enter_context(nc.sbuf_tensor("statlog", [P, 64], F32))
                   if "statlog" in rawtap else None)
        swb = {}
        dscwb = {}
        for wn in ("w1", "w2", "wg", "wu", "wd"):
            swb[wn] = es.enter_context(nc.sbuf_tensor("swb_" + wn, [P, 1], F32))
            dscwb[wn] = es.enter_context(nc.sbuf_tensor("dscw_" + wn, [P, 1], F32))

        streamT = lambda g: stream[:, D * g:D * (g + 1)]

        def rtap(name, sb):
            if name in rawtap:
                barrier()
                DG(rawtap[name], sb)
                barrier()

        def tap(name):
            if name in dbg:
                barrier()
                for g in range(NT):
                    DG(dbg[name][P * g:P * (g + 1), :], streamT(g))
                barrier()

        st_ln0 = {"tr": None, "rowB": None}

        def ln_stats(src_fn, width, w0=()):
            nch = width // 512
            tl = None
            for c in range(nch):
                tl = V(lambda e, c=c: e.bn_stats(out=bnb[:, 6 * c:6 * c + 6],
                                                 in_=src_fn(512 * c, 512 * (c + 1))),
                       waits=w0 if c == 0 else ())
            aw = [tl] + ([st_ln0["tr"]] if st_ln0["tr"] else [])
            return V(lambda e, n=nch: e.bn_aggr(out=aggrb[:, :], in_=bnb[:, 0:6 * n]),
                     waits=tuple(aw))

        def rstd_nmu(tok_aggr):
            tln = S(lambda e: e.activation(out=colb[:, 15:16], in_=aggrb[:, 1:2],
                                           func=AF.Ln, bias=epscol[:, :]),
                    waits=(tok_aggr,))
            tr = S(lambda e: e.activation(out=colb[:, RSTD:RSTD + 1],
                                          in_=colb[:, 15:16], func=AF.Exp,
                                          scale=-0.5), waits=(tln,))
            st_ln0["tr"] = tr
            Vn(lambda e: e.tensor_scalar(out=colb[:, NMU:NMU + 1], in0=aggrb[:, 0:1],
                                          scalar1=colb[:, RSTD:RSTD + 1], scalar2=-1.0,
                                          op0=ALU.mult, op1=ALU.mult), waits=(tr,))
            tn = V(lambda e: e.tensor_scalar(out=colb[:, NMUP:NMUP + 1], in0=aggrb[:, 0:1],
                                             scalar1=-1.0, scalar2=None, op0=ALU.mult))
            return tr, tn

        def ln_apply(dst, src, tok_nmu):
            return V(lambda e, d=dst, s=src: e.tensor_scalar(
                out=d, in0=s, scalar1=colb[:, NMUP:NMUP + 1], scalar2=colb[:, RSTD:RSTD + 1],
                op0=ALU.add, op1=ALU.mult), waits=(tok_nmu,))

        class WQ:

            def __init__(self, W, rows, name, dst, wqs, wqi, wpart, ps_q):
                self.W, self.name, self.dst = W, name, dst
                self.wqs, self.wqi, self.wpart, self.ps_q = wqs, wqi, wpart, ps_q
                ntile = rows // P
                self.cols = (8 * HD if rows == D else 32 * D) // ntile
                self.chunks = [(r, c) for r in range(ntile)
                               for c in range(self.cols // D)]
                self.i1 = 0
                self.i2 = 0
                self.prev = {}
                self.prev2 = {}
                self.tv2 = None

            def last_toks(self):
                d = dict(self.prev2 if self.prev2 else self.prev)
                return d

            def step1(self):
                if self.i1 < len(self.chunks):
                    idx = self.i1
                    r, c = self.chunks[idx]
                    half = slice(0, D) if idx % 2 == 0 else slice(D, 2 * D)
                    w0 = (self.prev[idx % 2],) if idx % 2 in self.prev else ()
                    td = DS(self.wqs[:, half],
                            self.W[P * r:P * (r + 1), D * c:D * (c + 1)], waits=w0)
                    self.prev[idx % 2] = V(
                        lambda e, h=half, i=idx, s=self: e.reduce_sum(
                            out=s.wpart[:, i:i + 1], in_=s.wqs[:, h], axis=AX.X,
                            apply_absolute_value=True), waits=(td,))
                    self.i1 += 1
                    return False
                tred = V(lambda e, s=self, n=len(self.chunks): e.reduce_sum(
                    out=colb[:, 10:11], in_=s.wpart[:, 0:n], axis=AX.X),
                    waits=tuple(self.prev.values()))
                tmm = T(lambda e, s=self: e.matmul(out=s.ps_q[0:1, :], lhsT=colb[:, 10:11],
                                                   rhs=srcT[:, 1:2], start=True, stop=True),
                        waits=(tred,))
                tms = V(lambda e, s=self: e.tensor_scalar(
                    out=sc2[0:1, 0:1], in0=s.ps_q[0:1, :], scalar1=1.0 / 4194304.0,
                    scalar2=1e-5, op0=ALU.mult, op1=ALU.max), waits=(tmm,))
                trc = V(lambda e: e.reciprocal(out=sc2[0:1, 1:2], in_=sc2[0:1, 0:1]),
                        waits=(tms,))
                tm1 = T(lambda e, s=self: e.matmul(out=s.ps_q[:, :], lhsT=ones_f[:, :],
                                                   rhs=sc2[0:1, 1:2], start=True, stop=True),
                        waits=(trc,))
                tv1 = V(lambda e, s=self: e.tensor_copy(out=swb[s.name][:, :],
                                                        in_=s.ps_q[:, :]), waits=(tm1,))
                tm2 = T(lambda e, s=self: e.matmul(out=s.ps_q[:, :], lhsT=ones_f[:, :],
                                                   rhs=sc2[0:1, 0:1], start=True, stop=True),
                        waits=(tv1,))
                self.tv2 = V(lambda e, s=self: e.tensor_copy(out=dscwb[s.name][:, :],
                                                             in_=s.ps_q[:, :]), waits=(tm2,))
                return True

            def step2(self):
                if self.i2 >= len(self.chunks):
                    return True
                idx = self.i2
                r, c = self.chunks[idx]
                half = slice(0, D) if idx % 2 == 0 else slice(D, 2 * D)
                w0 = [self.prev2[idx % 2]] if idx % 2 in self.prev2 else []
                if idx < 2 and self.tv2 is not None:
                    w0.append(self.tv2)
                td = DS(self.wqs[:, half],
                        self.W[P * r:P * (r + 1), D * c:D * (c + 1)], waits=tuple(w0))
                Vn(lambda e, h=half, s=self: e.tensor_scalar(
                    out=s.wqi[:, :], in0=s.wqs[:, h], scalar1=swb[s.name][:, :],
                    scalar2=None, op0=ALU.mult), waits=(td,))
                self.prev2[idx % 2] = V(
                    lambda e, s=self, r=r, c=c: e.tensor_scalar(
                        out=s.dst[:, s.cols * r + D * c:s.cols * r + D * (c + 1)],
                        in0=s.wqi[:, :], scalar1=-1.0, scalar2=1.0,
                        op0=ALU.max, op1=ALU.min))
                self.i2 += 1
                return self.i2 >= len(self.chunks)

        g_bo = contextlib.ExitStack()
        bo_b = g_bo.enter_context(nc.sbuf_tensor("bo_b", [P, D], F32))
        g_ht = contextlib.ExitStack()
        heads_tok = g_ht.enter_context(nc.sbuf_tensor("heads_tok", [P, NT * D], BF16))
        g_x0t = contextlib.ExitStack()
        X0T = g_x0t.enter_context(nc.sbuf_tensor("X0T", [P, 8 * L], BF16))
        X0Tr = X0T[:, :].rearrange("p (t i) -> p t i", i=L)

        g0 = contextlib.ExitStack()
        staging = g0.enter_context(nc.sbuf_tensor("staging", [P, HD], F32))
        brow2 = g0.enter_context(nc.sbuf_tensor("brow2", [1, D], F32))
        ps_bc = g0.enter_context(nc.psum_tensor("ps_bc", [P, D], F32))
        ps_tx0 = g0.enter_context(nc.psum_tensor("ps_tx0", [P, 512], F32))
        ps_tx1 = g0.enter_context(nc.psum_tensor("ps_tx1", [P, 512], F32))

        ld_x = [DS(streamT(g), X[P * g:P * (g + 1), :]) for g in range(NT)]
        t_idf = DS(identf[:, :], IDF)
        DS(identb[:, :], IDB)
        DS(cmask[:, :], CMASK)
        DS(bqkvT[:, :], BQKVT)
        t_bo = DS(brow2[0:1, 0:D], BOROW)
        Vn(lambda e: e.memset(ones_f[:, :], 1.0))
        Vn(lambda e: e.memset(ones_b[:, :], 1.0))
        Vn(lambda e: e.memset(srcT[:, 1:2], 1.0))
        Vn(lambda e: e.memset(epscol[:, :], EPS))
        t_cst = V(lambda e: e.memset(srcT[:, 0:1], 0.0))

        Tn(lambda e: e.matmul(out=ps_bc[:, 0:512], lhsT=ones_f[:, :], rhs=brow2[0:1, 0:512],
                              start=True, stop=True), waits=(t_bo, t_cst))
        tk = T(lambda e: e.matmul(out=ps_bc[:, 512:1024], lhsT=ones_f[:, :],
                                  rhs=brow2[0:1, 512:1024], start=True, stop=True))
        V(lambda e: e.tensor_copy(out=bo_b[:, :], in_=ps_bc[:, :]), waits=(tk,))

        ev_tok = {0: None, 1: None}
        for g in range(NT):
            for bb in range(2):
                bank = [ps_tx0, ps_tx1][bb]
                w0 = [ld_x[g], t_idf]
                if ev_tok[bb] is not None:
                    w0.append(ev_tok[bb])
                tb = None
                for k in range(4):
                    dt = 4 * bb + k
                    fn = (lambda e, g=g, d=dt, b=bank, k=k: e.transpose(
                        out=b[:, P * k:P * (k + 1)],
                        in_=stream[:, D * g + P * d:D * g + P * (d + 1)],
                        identity=identf[:, :]))
                    if k < 3:
                        Tn(fn, waits=tuple(w0) if k == 0 else ())
                    else:
                        tb = T(fn)
                ev_tok[bb] = V(lambda e, g=g, bb=bb, b=bank: e.tensor_copy(
                    out=X0Tr[:, 4 * bb:4 * bb + 4, P * g:P * (g + 1)],
                    in_=b[:, :].rearrange("p (t c) -> p t c", c=P)), waits=(tb,))
        close_phase()
        g0.close()

        barrier()
        g_va = contextlib.ExitStack()
        v_aug = g_va.enter_context(nc.sbuf_tensor("v_aug", [P, NT * 1040], BF16))
        v_augR = v_aug[:, :].rearrange("p (j h c) -> p j h c", h=NT, c=65)

        g_bv = contextlib.ExitStack()
        bv_b = g_bv.enter_context(nc.sbuf_tensor("bv_b", [P, D], F32))
        g1 = contextlib.ExitStack()
        wst = g1.enter_context(nc.sbuf_tensor("wst", [P, 2 * D], F32))
        wvb = g1.enter_context(nc.sbuf_tensor("wvb", [P, 8 * D], BF16))
        bvrow = g1.enter_context(nc.sbuf_tensor("sb_bvrow", [1, D], F32))
        ps_a = g1.enter_context(nc.psum_tensor("ps_a", [P, 512], F32))
        ps_b = g1.enter_context(nc.psum_tensor("ps_b", [P, 512], F32))
        t_bv = DS(bvrow[0:1, :], BVROW)
        Tn(lambda e: e.matmul(out=ps_a[:, :], lhsT=ones_f[:, :], rhs=bvrow[0:1, 0:512],
                              start=True, stop=True), waits=(t_bv,))
        tkv = T(lambda e: e.matmul(out=ps_b[:, :], lhsT=ones_f[:, :], rhs=bvrow[0:1, 512:D],
                                   start=True, stop=True))
        t_bvb = V(lambda e: e.tensor_copy(out=bv_b[:, 0:512], in_=ps_a[:, :]), waits=(tkv,))
        t_bvb = V(lambda e: e.tensor_copy(out=bv_b[:, 512:D], in_=ps_b[:, :]))

        wv_cast = []
        for dt in range(8):
            half = slice(0, D) if dt % 2 == 0 else slice(D, 2 * D)
            wwait = (wv_cast[dt - 2],) if dt >= 2 else ()
            td = DS(wst[:, half], WQKVT[P * dt:P * (dt + 1), 2 * D:3 * D], waits=wwait)
            wv_cast.append(V(lambda e, d=dt, h=half: e.tensor_copy(
                out=wvb[:, D * d:D * (d + 1)], in_=wst[:, h]), waits=(td,)))
        t_vones = V(lambda e: e.memset(v_augR[:, :, :, 64:65], 1.0))
        evv = {0: None, 1: None}
        for g in range(NT):
            for cc in range(2):
                bid = (2 * g + cc) % 2
                bank = [ps_a, ps_b][bid]
                w0 = [wv_cast[-1], t_vones, t_bvb] if evv[bid] is None else [evv[bid]]
                tmm = None
                for dt in range(8):
                    fn = (lambda e, g=g, c=cc, d=dt, b=bank: e.matmul(
                        out=b[:, :], lhsT=X0Tr[:, d, P * g:P * (g + 1)],
                        rhs=wvb[:, D * d + 512 * c:D * d + 512 * (c + 1)],
                        start=(d == 0), stop=(d == 7)))
                    if dt < 7:
                        Tn(fn, waits=tuple(w0) if dt == 0 else ())
                    else:
                        tmm = T(fn)
                evv[bid] = V(lambda e, g=g, c=cc, b=bank: e.tensor_tensor(
                    out=v_augR[:, g, 8 * c:8 * (c + 1), 0:64],
                    in0=b[:, :].rearrange("p (h c) -> p h c", c=64),
                    in1=bv_b[:, 512 * c:512 * (c + 1)].rearrange("p (h c) -> p h c", c=64),
                    op=ALU.add), waits=(tmm,))
        close_phase()
        g1.close()
        g_bv.close()

        barrier()
        g2 = contextlib.ExitStack()
        qTp = g2.enter_context(nc.sbuf_tensor("qTp", [P, 2 * L], BF16))
        kTp = g2.enter_context(nc.sbuf_tensor("kTp", [P, 2 * L], BF16))
        wst2 = g2.enter_context(nc.sbuf_tensor("wst2", [P, D], F32))
        wbb2 = g2.enter_context(nc.sbuf_tensor("wbb2", [P, 2 * D], BF16))
        E0 = g2.enter_context(nc.sbuf_tensor("E0", [P, 1024], BF16))
        E1 = g2.enter_context(nc.sbuf_tensor("E1", [P, 1024], BF16))
        rdb4 = g2.enter_context(nc.sbuf_tensor("rdb4", [P, 4], F32))
        wqs_c = g2.enter_context(nc.sbuf_tensor("wqs_c", [P, 2 * D], F32))
        wpart_c = bnb
        ps_s = [g2.enter_context(nc.psum_tensor("ps_s%d" % i, [P, 512], F32)) for i in range(4)]
        ps_av = [g2.enter_context(nc.psum_tensor("ps_av%d" % i, [P, 512], F32)) for i in range(2)]
        ps_qc = g2.enter_context(nc.psum_tensor("ps_qc", [P, 1], F32))
        ps_pj = g2.enter_context(nc.psum_tensor("ps_pj", [P, 512], F32))
        wq1o = WQ(W1T, D, "w1", None, wqs_c, None, wpart_c, ps_qc)
        wq2o = WQ(W2T, HD, "w2", None, wqs_c, None, wpart_c, ps_qc)
        wq_listA = [wq1o, wq2o]

        def wqA_tick(n):
            for _ in range(n):
                if not wq_listA:
                    return
                if wq_listA[0].step1():
                    done = wq_listA.pop(0)
                    if wq_listA:
                        wq_listA[0].prev = dict(done.prev)

        sbank_tok = {}
        ehalf_tok = {}
        avfree = {}
        t_cast2 = None
        t_projmm = None
        t_pairlast = {}
        t_proj = {}
        Ebuf = [E0, E1]

        def emit_proj_dma(p):
            nonlocal t_cast2, t_projmm
            dw = (t_cast2,) if t_cast2 else ()
            tdq = DS(wst2[:, :].rearrange("p (t c) -> p t c", c=P),
                     WQKVT[:, P * p:P * (p + 1)].rearrange("(t p) c -> p t c", p=P),
                     waits=dw)
            cwq = [tdq]
            if t_projmm is not None:
                cwq.append(t_projmm)
            tcq = V(lambda e: e.tensor_copy(out=wbb2[:, 0:D], in_=wst2[:, :]),
                    waits=tuple(cwq))
            tdk = DS(wst2[:, :].rearrange("p (t c) -> p t c", c=P),
                     WQKVT[:, D + P * p:D + P * (p + 1)].rearrange("(t p) c -> p t c", p=P),
                     waits=(tcq,))
            t_cast2 = V(lambda e: e.tensor_copy(out=wbb2[:, D:2 * D], in_=wst2[:, :]),
                        waits=(tdk,))

        def emit_proj_group(p, grp):
            nonlocal t_projmm
            qk, ic = grp // 4, grp % 4
            par_p = p % 2
            dstbuf = qTp if qk == 0 else kTp
            bcol = p if qk == 0 else 8 + p
            w0 = [t_cast2]
            if p >= 2 and par_p in t_pairlast:
                w0.append(t_pairlast[par_p])
            if t_proj.get((p, grp - 1)) is not None:
                w0 = [t_proj[(p, grp - 1)]]
            tmm = None
            for dt in range(8):
                fn = (lambda e, qk=qk, d=dt, i=ic: e.matmul(
                    out=ps_pj[:, :], lhsT=wbb2[:, D * qk + P * d:D * qk + P * (d + 1)],
                    rhs=X0Tr[:, d, 512 * i:512 * (i + 1)],
                    start=(d == 0), stop=(d == 7)))
                if dt < 7:
                    Tn(fn, waits=tuple(w0) if dt == 0 else ())
                else:
                    tmm = T(fn)
            t_projmm = tmm
            if qk == 0:
                tok = V(lambda e, d=dstbuf, i=ic, bc=bcol, pp=par_p: e.tensor_scalar(
                    out=d[:, L * pp + 512 * i:L * pp + 512 * (i + 1)],
                    in0=ps_pj[:, :], scalar1=bqkvT[:, bc:bc + 1],
                    scalar2=1.0 / 32.0, op0=ALU.add, op1=ALU.mult), waits=(tmm,))
            else:
                tok = V(lambda e, d=dstbuf, i=ic, bc=bcol, pp=par_p: e.tensor_scalar(
                    out=d[:, L * pp + 512 * i:L * pp + 512 * (i + 1)],
                    in0=ps_pj[:, :], scalar1=bqkvT[:, bc:bc + 1],
                    scalar2=None, op0=ALU.add), waits=(tmm,))
            t_proj[(p, grp)] = tok

        emit_proj_dma(0)
        for grp in range(8):
            emit_proj_group(0, grp)
        emit_proj_dma(1)

        for p in range(8):
            par_p = p % 2
            tproj_done = t_proj[(p, 7)]
            for ic in range(4):
                avjt = {}
                for jt in range(NT):
                    par = jt % 2
                    tok_sc = {}
                    for h01 in range(2):
                        bank = ps_s[2 * h01 + par]
                        rb = 64 * h01
                        w0 = [tproj_done]
                        if (h01, par) in sbank_tok:
                            w0.append(sbank_tok[(h01, par)])
                        tok_sc[h01] = T(lambda e, pp=par_p, h=h01, j=jt, i=ic, b=bank, r=rb:
                                        e.matmul(out=b[:, :],
                                                 lhsT=kTp[r:r + 64, L * pp + P * j:L * pp + P * (j + 1)],
                                                 rhs=qTp[r:r + 64, L * pp + 512 * i:L * pp + 512 * (i + 1)],
                                                 start=True, stop=True), waits=tuple(w0))
                    if ic == 3 and jt == NT - 1:
                        t_pairlast[par_p] = tok_sc[1]
                    for h01 in range(2):
                        bank = ps_s[2 * h01 + par]
                        w0 = [tok_sc[h01]]
                        if (h01, par) in ehalf_tok:
                            w0.append(ehalf_tok[(h01, par)])
                        sbank_tok[(h01, par)] = S(lambda e, h=h01, j=jt, b=bank: e.activation(
                            out=Ebuf[h][:, 512 * (j % 2):512 * (j % 2 + 1)], in_=b[:, :],
                            func=AF.Exp), waits=tuple(w0))
                    for h01 in range(2):
                        hglob = 2 * p + h01
                        w0 = [sbank_tok[(h01, par)]]
                        if jt == 0 and h01 in avfree:
                            w0.append(avfree[h01])
                        tlast = None
                        for it in range(4):
                            fn = (lambda e, h=h01, hg=hglob, j=jt, t=it: e.matmul(
                                out=ps_av[h][:, 65 * t:65 * (t + 1)],
                                lhsT=Ebuf[h][:, 512 * (j % 2) + P * t:512 * (j % 2) + P * (t + 1)],
                                rhs=v_augR[:, j, hg, :],
                                start=(j == 0 and t == 0), stop=(j == NT - 1 and t == 3),
                                skip_group_check=True))
                            if it < 3:
                                Tn(fn, waits=tuple(w0) if it == 0 else ())
                            else:
                                tlast = T(fn)
                        ehalf_tok[(h01, par)] = tlast
                        avjt[(h01, jt)] = tlast
                for h01 in range(2):
                    hglob = 2 * p + h01
                    avR = ps_av[h01][:, 0:260].rearrange("p (i c) -> p i c", c=65)
                    trd = V(lambda e, h=h01, a=avR: e.reciprocal(out=rdb4[:, :],
                                                                 in_=a[:, :, 64:65]),
                            waits=(avjt[(h01, NT - 1)],))
                    tn = None
                    for it in range(4):
                        g = 4 * ic + it
                        fn = (lambda e, h=h01, hg=hglob, a=avR, t=it, g=g: e.tensor_scalar(
                            out=heads_tok[:, D * g + 64 * hg:D * g + 64 * (hg + 1)],
                            in0=a[:, t, 0:64], scalar1=rdb4[:, t:t + 1], scalar2=None,
                            op0=ALU.mult))
                        if it < 3:
                            Vn(fn, waits=(trd,) if it == 0 else ())
                        else:
                            tn = V(fn)
                    avfree[h01] = tn
                wqA_tick(3)
                if p + 1 < 8:
                    emit_proj_group(p + 1, 2 * ic)
                    emit_proj_group(p + 1, 2 * ic + 1)
            if p + 2 < 8:
                emit_proj_dma(p + 2)
        wqA_tick(300)
        rtap("heads_tok", heads_tok[:, :])
        rtap("qTp", qTp[:, :])
        rtap("kTp", kTp[:, :])
        close_phase()
        g2.close()
        g_va.close()
        g_x0t.close()

        barrier()
        g2b = contextlib.ExitStack()
        headsT = g2b.enter_context(nc.sbuf_tensor("headsT", [P, 8 * L], BF16))
        headsTr = headsT[:, :].rearrange("p (c i) -> p c i", i=L)
        wo_b = g2b.enter_context(nc.sbuf_tensor("wo_b", [P, 8 * D], BF16))
        wst3 = g2b.enter_context(nc.sbuf_tensor("wst3", [P, 2 * D], F32))
        ps_txb = g2b.enter_context(nc.psum_tensor("ps_txb", [P, D], BF16))
        ps_o = g2b.enter_context(nc.psum_tensor("ps_o", [P, D], F32))

        wo_cast = []
        for ct in range(8):
            half = slice(0, D) if ct % 2 == 0 else slice(D, 2 * D)
            wwait = (wo_cast[ct - 2],) if ct >= 2 else ()
            td = DS(wst3[:, half], WOT[P * ct:P * (ct + 1), :], waits=wwait)
            wo_cast.append(V(lambda e, c=ct, h=half: e.tensor_copy(
                out=wo_b[:, D * c:D * (c + 1)], in_=wst3[:, h]), waits=(td,)))

        tx_ev = None
        po_ev = None
        for g in range(NT):
            w0 = [] if tx_ev is None else [tx_ev]
            tb = None
            for ct in range(8):
                fn = (lambda e, g=g, c=ct: e.transpose(
                    out=ps_txb[:, P * c:P * (c + 1)],
                    in_=heads_tok[:, D * g + P * c:D * g + P * (c + 1)],
                    identity=identb[:, :]))
                if ct < 7:
                    Tn(fn, waits=tuple(w0) if ct == 0 else ())
                else:
                    tb = T(fn)
            tx_ev = V(lambda e, g=g: e.tensor_copy(
                out=headsTr[:, 0:8, P * g:P * (g + 1)],
                in_=ps_txb[:, :].rearrange("p (c i) -> p c i", i=P)), waits=(tb,))
        for g in range(NT):
            tmm = None
            for ec in range(2):
                w0 = [tx_ev, wo_cast[-1]] if (g == 0 and ec == 0) else []
                if po_ev is not None and ec == 0:
                    w0 = [po_ev]
                for ct in range(8):
                    fn = (lambda e, g=g, c=ct, x=ec: e.matmul(
                        out=ps_o[:, 512 * x:512 * (x + 1)],
                        lhsT=headsTr[:, c, P * g:P * (g + 1)],
                        rhs=wo_b[:, D * c + 512 * x:D * c + 512 * (x + 1)],
                        start=(c == 0), stop=(c == 7)))
                    if ct < 7:
                        Tn(fn, waits=tuple(w0) if ct == 0 else ())
                    else:
                        tmm = T(fn)
            Vn(lambda e, g=g: e.scalar_tensor_tensor(
                out=scr1[:, :], in0=streamT(g), scalar=2.0, in1=bo_b[:, :],
                op0=ALU.mult, op1=ALU.add), waits=(tmm,))
            tadd = V(lambda e: e.tensor_tensor(out=scr1[:, :], in0=scr1[:, :],
                                               in1=ps_o[:, :], op=ALU.add))
            po_ev = tadd
            ta = ln_stats(lambda c0, c1: scr1[:, c0:c1], D)
            _, tnm = rstd_nmu(ta)
            if statlog is not None:
                Vn(lambda e, g=g: e.tensor_copy(out=statlog[:, 4 * g:4 * g + 2],
                                                in_=aggrb[:, :]), waits=(tnm,))
                Vn(lambda e, g=g: e.tensor_copy(out=statlog[:, 4 * g + 2:4 * g + 3],
                                                in_=colb[:, RSTD:RSTD + 1]))
                Vn(lambda e, g=g: e.tensor_copy(out=statlog[:, 4 * g + 3:4 * g + 4],
                                                in_=colb[:, NMUP:NMUP + 1]))
            ln_apply(streamT(g), scr1[:, :], tnm)
        if statlog is not None:
            rtap("statlog", statlog[:, :])
        close_phase()
        g2b.close()
        g_ht.close()
        g_bo.close()
        tap("l0attn")
        if n_phases <= 2:
            _emit_out(barrier, DG, C, cur, close_phase, stream, Y)
            return nc

        barrier()
        g_l0c = contextlib.ExitStack()
        cln1 = g_l0c.enter_context(nc.sbuf_tensor("cln1", [2, D], F16))
        cln2 = g_l0c.enter_context(nc.sbuf_tensor("cln2", [2, HD], F16))
        cb1 = g_l0c.enter_context(nc.sbuf_tensor("cb1", [1, HD], F16))
        cb2 = g_l0c.enter_context(nc.sbuf_tensor("cb2", [1, D], F16))
        gst = contextlib.ExitStack()
        stg2 = gst.enter_context(nc.sbuf_tensor("stg2", [2, HD], F32))
        t_r1 = DS(stg2[0:2, 0:D], LN1GB)
        tc1 = V(lambda e: e.tensor_copy(out=cln1[:, :], in_=stg2[0:2, 0:D]), waits=(t_r1,))
        t_r2 = DS(stg2[0:2, 0:HD], LN2GB, waits=(tc1,))
        tc2 = V(lambda e: e.tensor_copy(out=cln2[:, :], in_=stg2[0:2, 0:HD]), waits=(t_r2,))
        t_r3 = DS(stg2[0:1, 0:HD], B1ROW, waits=(tc2,))
        tc3 = V(lambda e: e.tensor_copy(out=cb1[:, :], in_=stg2[0:1, 0:HD]), waits=(t_r3,))
        t_r4 = DS(stg2[0:1, 0:D], B2ROW, waits=(tc3,))
        Vn(lambda e: e.tensor_copy(out=cb2[:, :], in_=stg2[0:1, 0:D]), waits=(t_r4,))
        close_phase()
        gst.close()
        g_w12 = contextlib.ExitStack()
        w1q = g_w12.enter_context(nc.sbuf_tensor("w1q", [P, 8 * HD], FP8))
        w2q = g_w12.enter_context(nc.sbuf_tensor("w2q", [P, 32 * D], FP8))


        gqs = contextlib.ExitStack()
        wqs_a = gqs.enter_context(nc.sbuf_tensor("wqs_a", [P, 2 * D], F32))
        wqi_a = gqs.enter_context(nc.sbuf_tensor("wqi_a", [P, D], I32))
        prev_rounds = {}
        for o, d in ((wq1o, w1q), (wq2o, w2q)):
            o.wqs = wqs_a
            o.wqi = wqi_a
            o.dst = d
            o.prev2 = dict(prev_rounds)
            while not o.step2():
                pass
            prev_rounds = dict(o.prev2)
        barrier()
        close_phase()
        gqs.close()

        g_x1t = contextlib.ExitStack()
        g3 = contextlib.ExitStack()
        gbuf = g3.enter_context(nc.sbuf_tensor("gbuf", [P, HD], F32))
        xq = g3.enter_context(nc.sbuf_tensor("xq", [P, D], BF16))
        xqT = g3.enter_context(nc.sbuf_tensor("xqT", [P, D], BF16))
        h2c = g3.enter_context(nc.sbuf_tensor("h2c", [P, 512], F32))
        h2qb = g3.enter_context(nc.sbuf_tensor("h2qb", [P, 512], BF16))
        h2qT = g3.enter_context(nc.sbuf_tensor("h2qT", [P, HD], BF16))
        ps_r = g3.enter_context(nc.psum_tensor("ps_r", [2, P], F32))
        ps_A = g3.enter_context(nc.psum_tensor("ps_A", [P, 512], F32))
        ps_B = g3.enter_context(nc.psum_tensor("ps_B", [P, 512], F32))
        ps_1a = g3.enter_context(nc.psum_tensor("ps_1a", [P, 512], F32))
        ps_1b = g3.enter_context(nc.psum_tensor("ps_1b", [P, 512], F32))
        ps_2 = g3.enter_context(nc.psum_tensor("ps_2", [P, 512], F32))
        ps_txq = g3.enter_context(nc.psum_tensor("ps_txq", [P, D], BF16))

        def mkrow(dst, src_col, k, w0):
            tt = T(lambda e, s=src_col, k=k: e.transpose(out=ps_r[0:k, :], in_=s,
                                                         identity=identf[:, :]),
                   waits=tuple(w0))
            return V(lambda e, d=dst, k=k: e.tensor_copy(out=d, in_=ps_r[0:k, :]), waits=(tt,))

        prev_gelu = {}
        prev_ap = None
        prev_w2 = None
        prev_tx = None
        for g in range(NT):
            ta = ln_stats(lambda c0, c1, g=g: streamT(g)[:, c0:c1], D,
                          w0=(prev_w2,) if prev_w2 else ())
            trs, tnm = rstd_nmu(ta)
            trowA = mkrow(rowA[:, :], colb[:, RSTD:RSTD + 1], 1, [trs] +
                          ([prev_ap] if prev_ap else []))
            sw0 = [tnm] + ([st_ln0["rowB"]] if st_ln0["rowB"] else [])
            tsr = V(lambda e: e.tensor_copy(out=srcT[:, 0:1], in_=colb[:, NMU:NMU + 1]),
                    waits=tuple(sw0))
            trowB = mkrow(rowB[:, :], srcT[:, :], 2, [tsr, trowA])
            st_ln0["rowB"] = trowB
            tap_tok = None
            for dc in range(2):
                tA = T(lambda e, d=dc: e.matmul(out=ps_A[:, :], lhsT=rowA[:, :],
                                                rhs=cln1[0:1, 512 * d:512 * (d + 1)],
                                                start=True, stop=True),
                       waits=(trowA, trowB) if dc == 0 else (tap_tok,))
                tB = T(lambda e, d=dc: e.matmul(out=ps_B[:, :], lhsT=rowB[:, :],
                                                rhs=cln1[0:2, 512 * d:512 * (d + 1)],
                                                start=True, stop=True))
                Vn(lambda e, g=g, d=dc: e.tensor_tensor(
                    out=gbuf[:, 512 * d:512 * (d + 1)],
                    in0=streamT(g)[:, 512 * d:512 * (d + 1)], in1=ps_A[:, :], op=ALU.mult),
                   waits=(tB,))
                tap_tok = V(lambda e, d=dc: e.tensor_tensor(
                    out=gbuf[:, 512 * d:512 * (d + 1)], in0=gbuf[:, 512 * d:512 * (d + 1)],
                    in1=ps_B[:, :], op=ALU.add))
            prev_ap = tap_tok
            t1 = V(lambda e: e.reduce_max(out=colb[:, RM:RM + 1], in_=gbuf[:, 0:D],
                                          axis=AX.X, apply_absolute_value=True),
                   waits=(tap_tok,))
            t2 = V(lambda e: e.tensor_scalar(out=colb[:, MCL:MCL + 1], in0=colb[:, RM:RM + 1],
                                             scalar1=1e-5, scalar2=None, op0=ALU.max),
                   waits=(t1,))
            t3 = V(lambda e: e.reciprocal(out=colb[:, RC:RC + 1], in_=colb[:, MCL:MCL + 1]),
                   waits=(t2,))
            tsx = V(lambda e: e.tensor_scalar(out=colb[:, SX:SX + 1], in0=colb[:, RC:RC + 1],
                                              scalar1=127.0, scalar2=None, op0=ALU.mult),
                    waits=(t3,))
            Vn(lambda e: e.tensor_scalar(out=colb[:, DSC:DSC + 1], in0=colb[:, MCL:MCL + 1],
                                         scalar1=1.0 / 127.0, scalar2=dscwb["w1"][:, :],
                                         op0=ALU.mult, op1=ALU.mult))
            tsxw = V(lambda e: e.tensor_scalar(out=colb[:, SXW:SXW + 1], in0=colb[:, SX:SX + 1],
                                               scalar1=swb["w1"][:, :], scalar2=None,
                                               op0=ALU.mult))
            trAug = mkrow(rowAug[:, :], colb[:, SXW:SXW + 1], 1, [tsxw, trowB])
            Vn(lambda e: e.tensor_scalar(out=gbuf[:, D:2 * D].bitcast(I32), in0=gbuf[:, 0:D],
                                         scalar1=colb[:, SX:SX + 1], scalar2=None,
                                         op0=ALU.mult), waits=(tsx,))
            txq = V(lambda e: e.tensor_copy(out=xq[:, :], in_=gbuf[:, D:2 * D].bitcast(I32)))
            w0 = [txq] + ([prev_tx] if prev_tx else [])
            tb = None
            for dt in range(8):
                fn = (lambda e, d=dt: e.transpose(out=ps_txq[:, P * d:P * (d + 1)],
                                                  in_=xq[:, P * d:P * (d + 1)],
                                                  identity=identb[:, :]))
                if dt < 7:
                    Tn(fn, waits=tuple(w0) if dt == 0 else ())
                else:
                    tb = T(fn)
            prev_tx = V(lambda e: e.tensor_copy(out=xqT[:, :], in_=ps_txq[:, :]), waits=(tb,))
            for hc in range(8):
                bank = [ps_1a, ps_1b][hc % 2]
                w0 = [prev_tx]
                if (hc % 2) in prev_gelu:
                    w0.append(prev_gelu[hc % 2])
                tmm = None
                for dt in range(8):
                    fn = (lambda e, d=dt, h=hc, b=bank: e.matmul(
                        out=b[:, :], lhsT=xqT[:, P * d:P * (d + 1)],
                        rhs=w1q[:, HD * d + 512 * h:HD * d + 512 * (h + 1)],
                        start=(d == 0), stop=False))
                    Tn(fn, waits=tuple(w0) if dt == 0 else ())
                tmm = T(lambda e, h=hc, b=bank: e.matmul(
                    out=b[:, :], lhsT=rowAug[:, :], rhs=cb1[0:1, 512 * h:512 * (h + 1)],
                    start=False, stop=True), waits=(trAug,))
                tg = S(lambda e, h=hc, b=bank: e.activation(
                    out=gbuf[:, 512 * h:512 * (h + 1)], in_=b[:, :], func=AF.Gelu,
                    scale=colb[:, DSC:DSC + 1]), waits=(tmm,))
                prev_gelu[hc % 2] = V(lambda e, h=hc: e.bn_stats(
                    out=bnb[:, 6 * h:6 * h + 6], in_=gbuf[:, 512 * h:512 * (h + 1)]),
                    waits=(tg,))
            ta2 = V(lambda e: e.bn_aggr(out=aggrb[:, :], in_=bnb[:, 0:48]),
                    waits=(prev_gelu[0], prev_gelu[1]))
            trs2, tnm2 = rstd_nmu(ta2)
            trowA = mkrow(rowA[:, :], colb[:, RSTD:RSTD + 1], 1, [trs2, trAug])
            sw0 = [tnm2] + ([st_ln0["rowB"]] if st_ln0["rowB"] else [])
            tsr = V(lambda e: e.tensor_copy(out=srcT[:, 0:1], in_=colb[:, NMU:NMU + 1]),
                    waits=tuple(sw0))
            trowB = mkrow(rowB[:, :], srcT[:, :], 2, [tsr, trowA])
            st_ln0["rowB"] = trowB
            tmax = None
            tap2 = None
            for hc in range(8):
                tA = T(lambda e, h=hc: e.matmul(out=ps_A[:, :], lhsT=rowA[:, :],
                                                rhs=cln2[0:1, 512 * h:512 * (h + 1)],
                                                start=True, stop=True),
                       waits=(trowA, trowB) if hc == 0 else (tap2,))
                tB = T(lambda e, h=hc: e.matmul(out=ps_B[:, :], lhsT=rowB[:, :],
                                                rhs=cln2[0:2, 512 * h:512 * (h + 1)],
                                                start=True, stop=True))
                Vn(lambda e, h=hc: e.tensor_tensor(out=h2c[:, :],
                                                   in0=gbuf[:, 512 * h:512 * (h + 1)],
                                                   in1=ps_A[:, :], op=ALU.mult), waits=(tB,))
                tap2 = V(lambda e, h=hc: e.tensor_tensor(
                    out=gbuf[:, 512 * h:512 * (h + 1)], in0=h2c[:, :], in1=ps_B[:, :],
                    op=ALU.add))
                tmax = V(lambda e, h=hc: e.reduce_max(
                    out=mx8[:, h:h + 1], in_=gbuf[:, 512 * h:512 * (h + 1)], axis=AX.X,
                    apply_absolute_value=True))
            t1 = V(lambda e: e.reduce_max(out=colb[:, RM:RM + 1], in_=mx8[:, :], axis=AX.X),
                   waits=(tmax,))
            t2 = V(lambda e: e.tensor_scalar(out=colb[:, MCL:MCL + 1], in0=colb[:, RM:RM + 1],
                                             scalar1=1e-5, scalar2=None, op0=ALU.max),
                   waits=(t1,))
            t3 = V(lambda e: e.reciprocal(out=colb[:, RC:RC + 1], in_=colb[:, MCL:MCL + 1]),
                   waits=(t2,))
            tsx = V(lambda e: e.tensor_scalar(out=colb[:, SX:SX + 1], in0=colb[:, RC:RC + 1],
                                              scalar1=127.0, scalar2=None, op0=ALU.mult),
                    waits=(t3,))
            Vn(lambda e: e.tensor_scalar(out=colb[:, DSC2:DSC2 + 1], in0=colb[:, MCL:MCL + 1],
                                         scalar1=1.0 / 127.0, scalar2=dscwb["w2"][:, :],
                                         op0=ALU.mult, op1=ALU.mult))
            tsxw = V(lambda e: e.tensor_scalar(out=colb[:, SXW:SXW + 1], in0=colb[:, SX:SX + 1],
                                               scalar1=swb["w2"][:, :], scalar2=None,
                                               op0=ALU.mult))
            trAug = mkrow(rowAug[:, :], colb[:, SXW:SXW + 1], 1, [tsxw, trowB])
            tq_ev = None
            for hc in range(8):
                Vn(lambda e, h=hc: e.tensor_scalar(
                    out=h2c[:, :].bitcast(I32), in0=gbuf[:, 512 * h:512 * (h + 1)],
                    scalar1=colb[:, SX:SX + 1], scalar2=None, op0=ALU.mult),
                   waits=(tsx,) if hc == 0 else ())
                tqb = V(lambda e: e.tensor_copy(out=h2qb[:, :], in_=h2c[:, :].bitcast(I32)))
                w0 = [tqb] + ([tq_ev] if tq_ev else [prev_tx])
                tb = None
                for k in range(4):
                    fn = (lambda e, k=k: e.transpose(
                        out=ps_txq[:, P * k:P * (k + 1)],
                        in_=h2qb[:, P * k:P * (k + 1)], identity=identb[:, :]))
                    if k < 3:
                        Tn(fn, waits=tuple(w0) if k == 0 else ())
                    else:
                        tb = T(fn)
                tq_ev = V(lambda e, h=hc: e.tensor_copy(
                    out=h2qT[:, 512 * h:512 * (h + 1)], in_=ps_txq[:, 0:512]), waits=(tb,))
            prev_tx = tq_ev
            tstt = None
            for ec in range(2):
                w0 = [tq_ev] if ec == 0 else [tstt]
                tmm = None
                for ht in range(32):
                    fn = (lambda e, t=ht, x=ec: e.matmul(
                        out=ps_2[:, :], lhsT=h2qT[:, P * t:P * (t + 1)],
                        rhs=w2q[:, D * t + 512 * x:D * t + 512 * (x + 1)],
                        start=(t == 0), stop=False))
                    Tn(fn, waits=tuple(w0) if ht == 0 else ())
                tmm = T(lambda e, x=ec: e.matmul(
                    out=ps_2[:, :], lhsT=rowAug[:, :], rhs=cb2[0:1, 512 * x:512 * (x + 1)],
                    start=False, stop=True), waits=(trAug,))
                tstt = V(lambda e, g=g, x=ec: e.scalar_tensor_tensor(
                    out=scr1[:, 512 * x:512 * (x + 1)], in0=ps_2[:, :],
                    scalar=colb[:, DSC2:DSC2 + 1],
                    in1=streamT(g)[:, 512 * x:512 * (x + 1)],
                    op0=ALU.mult, op1=ALU.add), waits=(tmm,))
            ta3 = ln_stats(lambda c0, c1: scr1[:, c0:c1], D, w0=(tstt,))
            _, tnm3 = rstd_nmu(ta3)
            prev_w2 = ln_apply(streamT(g), scr1[:, :], tnm3)
        close_phase()
        g3.close()
        g_w12.close()
        g_l0c.close()
        tap("l0ffn")
        if n_phases <= 3:
            _emit_out(barrier, DG, C, cur, close_phase, stream, Y)
            return nc

        barrier()
        g4pre = contextlib.ExitStack()
        qTb = g4pre.enter_context(nc.sbuf_tensor("qTb", [64, L], BF16))
        wqs_b = g4pre.enter_context(nc.sbuf_tensor("wqs_b", [P, 2 * D], F32))
        wqi_b = scr1[:, :].bitcast(I32)
        wpart_b = bnb
        ktok = g4pre.enter_context(nc.sbuf_tensor("ktok", [P, NT * 64], F32))
        vtok = g4pre.enter_context(nc.sbuf_tensor("vtok", [P, NT * P], BF16))
        vtokR = vtok[:, :].rearrange("p (j h c) -> p j h c", h=64, c=2)
        EL0 = g4pre.enter_context(nc.sbuf_tensor("EL0", [P, L], BF16))
        EL1 = g4pre.enter_context(nc.sbuf_tensor("EL1", [P, L], BF16))
        qrow = g4pre.enter_context(nc.sbuf_tensor("qrow", [1, L], BF16))
        hn_tok = g4pre.enter_context(nc.sbuf_tensor("hn_tok", [P, NT * 64], BF16))
        rdb = g4pre.enter_context(nc.sbuf_tensor("rdb", [P, 64], F32))

        g4v = contextlib.ExitStack()
        vkf = g4v.enter_context(nc.sbuf_tensor("vkf", [P, L], F32))
        g4x = contextlib.ExitStack()
        X1T = g4x.enter_context(nc.sbuf_tensor("X1T", [P, 8 * L], BF16))
        X1Tr = X1T[:, :].rearrange("p (t i) -> p t i", i=L)
        wq1b = g4x.enter_context(nc.sbuf_tensor("wq1b", [P, 1536], BF16))
        wq1s = g4x.enter_context(nc.sbuf_tensor("wq1s", [P, 1536], F32))
        g4a = contextlib.ExitStack()
        ps_f0 = g4a.enter_context(nc.psum_tensor("ps_f0", [P, 512], F32))
        ps_f1 = g4a.enter_context(nc.psum_tensor("ps_f1", [P, 512], F32))
        ps_q4 = g4a.enter_context(nc.psum_tensor("ps_q4", [P, 1], F32))

        wq_g = WQ(WGT, D, "wg", None, wqs_b, wqi_b, wpart_b, ps_q4)
        wq_u = WQ(WUT, D, "wu", None, wqs_b, wqi_b, wpart_b, ps_q4)
        wq_d = WQ(WDT, HD, "wd", None, wqs_b, wqi_b, wpart_b, ps_q4)
        wq_list1 = [wq_g, wq_u, wq_d]

        def wq1_tick(n):
            for _ in range(n):
                if not wq_list1:
                    return
                if wq_list1[0].step1():
                    done = wq_list1.pop(0)
                    if wq_list1:
                        wq_list1[0].prev = dict(done.prev)

        ev_tok = {0: None, 1: None}
        for g in range(NT):
            for bb in range(2):
                bank = [ps_f0, ps_f1][bb]
                w0 = [] if ev_tok[bb] is None else [ev_tok[bb]]
                tb = None
                for k in range(4):
                    dt = 4 * bb + k
                    fn = (lambda e, g=g, d=dt, b=bank, k=k: e.transpose(
                        out=b[:, P * k:P * (k + 1)],
                        in_=stream[:, D * g + P * d:D * g + P * (d + 1)],
                        identity=identf[:, :]))
                    if k < 3:
                        Tn(fn, waits=tuple(w0) if k == 0 else ())
                    else:
                        tb = T(fn)
                ev_tok[bb] = V(lambda e, g=g, bb=bb, b=bank: e.tensor_copy(
                    out=X1Tr[:, 4 * bb:4 * bb + 4, P * g:P * (g + 1)],
                    in_=b[:, :].rearrange("p (t c) -> p t c", c=P)), waits=(tb,))
            wq1_tick(4)
        td = DS(wq1s[:, :].rearrange("p (t c) -> p t c", c=192),
                WQKV1T[:, :].rearrange("(t p) c -> p t c", p=P))
        tcst = V(lambda e: e.tensor_copy(out=wq1b[:, :], in_=wq1s[:, :]), waits=(td,))
        evq = {0: None, 1: None}
        for fc in range(2):
            fw = P if fc == 0 else 64
            for ic in range(4):
                bid = ic % 2
                bank = [ps_f0, ps_f1][bid]
                w0 = [tcst, ev_tok[0], ev_tok[1]] if evq[bid] is None else [evq[bid]]
                tmm = None
                for dt in range(8):
                    fn = (lambda e, f=fc, d=dt, i=ic, b=bank, fw=fw: e.matmul(
                        out=b[0:fw, :], lhsT=wq1b[:, 192 * d + P * f:192 * d + P * f + fw],
                        rhs=X1Tr[:, d, 512 * i:512 * (i + 1)],
                        start=(d == 0), stop=(d == 7)))
                    if dt < 7:
                        Tn(fn, waits=tuple(w0) if dt == 0 else ())
                    else:
                        tmm = T(fn)
                if fc == 0:
                    Vn(lambda e, i=ic, b=bank: e.tensor_copy(
                        out=qTb[0:64, 512 * i:512 * (i + 1)], in_=b[0:64, :]), waits=(tmm,))
                    evq[bid] = V(lambda e, i=ic, b=bank: e.tensor_copy(
                        out=vkf[64:P, 512 * i:512 * (i + 1)], in_=b[64:P, :]))
                else:
                    evq[bid] = V(lambda e, i=ic, b=bank: e.tensor_copy(
                        out=vkf[0:64, 512 * i:512 * (i + 1)], in_=b[0:64, :]), waits=(tmm,))
                wq1_tick(4)
        wq1_tick(300)
        close_phase()
        g4a.close()
        g4x.close()

        barrier()
        g4d = contextlib.ExitStack()
        ps_k = g4d.enter_context(nc.psum_tensor("ps_k", [P, 64], F32))
        ps_v = g4d.enter_context(nc.psum_tensor("ps_v", [P, 64], F32))
        Vn(lambda e: e.memset(vtokR[:, :, :, 1:2], 1.0))
        tk_ev = None
        tv_ev = None
        for jt in range(NT):
            w0 = [] if tk_ev is None else [tk_ev]
            tkk = T(lambda e, j=jt: e.transpose(out=ps_k[:, :],
                                                in_=vkf[64:P, P * j:P * (j + 1)],
                                                identity=identf[64:P, 64:P]),
                    waits=tuple(w0))
            tk_ev = V(lambda e, j=jt: e.tensor_scalar(
                out=ktok[:, 64 * j:64 * (j + 1)], in0=ps_k[:, :], scalar1=0.125,
                scalar2=None, op0=ALU.mult), waits=(tkk,))
            w0 = [] if tv_ev is None else [tv_ev]
            tvv = T(lambda e, j=jt: e.transpose(out=ps_v[:, :],
                                                in_=vkf[0:64, P * j:P * (j + 1)],
                                                identity=identf[0:64, 0:64]),
                    waits=tuple(w0))
            tv_ev = V(lambda e, j=jt: e.tensor_copy(
                out=vtokR[:, j, :, 0:1], in_=ps_v[:, :].rearrange("p (h c) -> p h c", c=1)),
                waits=(tvv,))
        close_phase()
        g4d.close()

        g4v.close()

        g5w = contextlib.ExitStack()
        wgq = g5w.enter_context(nc.sbuf_tensor("wgq", [P, 8 * HD], FP8))
        wuq = g5w.enter_context(nc.sbuf_tensor("wuq", [P, 8 * HD], FP8))
        wdq = g5w.enter_context(nc.sbuf_tensor("wdq", [P, 32 * D], FP8))
        wq_g.dst = wgq
        wq_u.dst = wuq
        wq_d.dst = wdq
        wq_list2 = [wq_g, wq_u, wq_d]

        def wq2_tick(n):
            for _ in range(n):
                if not wq_list2:
                    return
                if wq_list2[0].step2():
                    done = wq_list2.pop(0)
                    if wq_list2:
                        wq_list2[0].prev2 = dict(done.prev2)

        barrier()
        g4e = contextlib.ExitStack()
        ps_qb = g4e.enter_context(nc.psum_tensor("ps_qb", [P, 2 * D], F32))
        ps_avL = g4e.enter_context(nc.psum_tensor("ps_avL", [P, 2 * D], F32))
        ps_avR = ps_avL[:, :].rearrange("p (i h c) -> p i h c", h=64, c=2)
        Ebufs = [EL0, EL1]
        t_qrow = {}
        t_qbc = {}
        t_expdone = {}
        t_avpar = {}
        tlastmm = None
        for h in range(64):
            w0 = [t_qbc[h - 1]] if h >= 1 else []
            t_qrow[h] = DG(qrow[0:1, :], qTb[h:h + 1, :], waits=tuple(w0))
            w0 = [t_qrow[h]]
            if h >= 1:
                w0.append(t_expdone[h - 1])
            tqb = None
            for ic in range(4):
                fn = (lambda e, i=ic: e.matmul(
                    out=ps_qb[:, 512 * i:512 * (i + 1)], lhsT=ones_b[:, :],
                    rhs=qrow[0:1, 512 * i:512 * (i + 1)], start=True, stop=True))
                if ic < 3:
                    Tn(fn, waits=tuple(w0) if ic == 0 else ())
                else:
                    tqb = T(fn)
            t_qbc[h] = tqb
            tle = None
            for jt in range(NT):
                par = jt % 2
                Eb = Ebufs[par]
                i0 = P * jt
                w0 = [tqb]
                if par in t_avpar:
                    w0.append(t_avpar[par])
                insts = [(i0, D), (D, L)] if i0 < D else [(i0, L)]
                tlast = None
                for (ia, ib) in insts:
                    tlast = S(lambda e, h=h, Eb=Eb, j=jt, ia=ia, ib=ib: e.activation(
                        out=Eb[:, ia:ib], in_=ps_qb[:, ia:ib], func=AF.Exp,
                        scale=ktok[:, 64 * j + h:64 * j + h + 1]),
                        waits=tuple(w0) if (ia, ib) == insts[0] else ())
                tle = tlast
                tmsk = V(lambda e, Eb=Eb, j=jt: e.tensor_tensor(
                    out=Eb[:, P * j:P * (j + 1)], in0=Eb[:, P * j:P * (j + 1)],
                    in1=cmask[:, :], op=ALU.mult), waits=(tlast,))
                for it in range(jt, NT):
                    st = (h == 0 and jt == 0 and it % 4 == 0)
                    w1 = [tmsk] if it == jt else ()
                    fn = (lambda e, Eb=Eb, h=h, j=jt, t=it, st=st: e.matmul(
                        out=ps_avL[:, P * t + 2 * h:P * t + 2 * (h + 1)],
                        lhsT=Eb[:, P * t:P * (t + 1)],
                        rhs=vtokR[:, j, h, :],
                        start=st, stop=(h == 63 and j == t),
                        skip_group_check=(not st)))
                    if it == NT - 1:
                        tlastmm = T(fn, waits=tuple(w1))
                        t_avpar[par] = tlastmm
                    else:
                        Tn(fn, waits=tuple(w1))
            t_expdone[h] = tle
            wq2_tick(2)
        wq2_tick(300)
        t_avdone = {63: tlastmm}
        g4f = contextlib.ExitStack()
        hnT = g4f.enter_context(nc.sbuf_tensor("hnT", [64, L], BF16))
        wo1b = g4f.enter_context(nc.sbuf_tensor("wo1b", [64, D], BF16))
        wo1s = wqs_b[0:64, 0:D]
        thn = {}
        for it in range(NT):
            trd = V(lambda e, t=it: e.reciprocal(out=rdb[:, :], in_=ps_avR[:, t, :, 1:2]),
                    waits=(t_avdone[63],))
            thn[it] = V(lambda e, t=it: e.tensor_tensor(
                out=hn_tok[:, 64 * t:64 * (t + 1)],
                in0=ps_avR[:, t, :, 0], in1=rdb[:, :], op=ALU.mult), waits=(trd,))
        close_phase()
        g4e.close()

        barrier()
        g4g = contextlib.ExitStack()
        ps_ht = g4g.enter_context(nc.psum_tensor("ps_ht", [64, 512], BF16))
        ps_so = g4g.enter_context(nc.psum_tensor("ps_so", [P, D], F32))
        td = DS(wo1s, WOUT1T, waits=(("vS", C["vS"]),))
        tw1o = V(lambda e: e.tensor_copy(out=wo1b[:, :], in_=wo1s), waits=(td,))
        tx_ev = None
        for bb in range(4):
            w0 = [] if tx_ev is None else [tx_ev]
            tb = None
            for k in range(4):
                it = 4 * bb + k
                fn = (lambda e, t=it, k=k: e.transpose(
                    out=ps_ht[:, P * k:P * (k + 1)], in_=hn_tok[:, 64 * t:64 * (t + 1)],
                    identity=identb[:, :]))
                if k < 3:
                    Tn(fn, waits=tuple(w0) if k == 0 else ())
                else:
                    tb = T(fn)
            tx_ev = V(lambda e, bb=bb: e.tensor_copy(
                out=hnT[:, 512 * bb:512 * (bb + 1)], in_=ps_ht[:, :]), waits=(tb,))
        t_ln = None
        for g in range(NT):
            w0 = [tx_ev, tw1o] if g == 0 else [t_ln]
            tmm = None
            for ec in range(2):
                fn = (lambda e, g=g, x=ec: e.matmul(
                    out=ps_so[:, 512 * x:512 * (x + 1)], lhsT=hnT[:, P * g:P * (g + 1)],
                    rhs=wo1b[:, 512 * x:512 * (x + 1)], start=True, stop=True))
                if ec == 0:
                    Tn(fn, waits=tuple(w0))
                else:
                    tmm = T(fn)
            tadd = V(lambda e, g=g: e.tensor_tensor(out=scr1[:, :], in0=ps_so[:, :],
                                                    in1=streamT(g), op=ALU.add),
                     waits=(tmm,))
            ta = ln_stats(lambda c0, c1: scr1[:, c0:c1], D)
            _, tnm = rstd_nmu(ta)
            t_ln = ln_apply(streamT(g), scr1[:, :], tnm)
        close_phase()
        g4g.close()
        g4f.close()
        tap("l1attn")
        if n_phases <= 4:
            _emit_out(barrier, DG, C, cur, close_phase, stream, Y)
            return nc

        barrier()

        g5 = contextlib.ExitStack()
        gu_half = [EL0[:, :].bitcast(F32), EL1[:, :].bitcast(F32),
                   ktok[:, :], vtok[:, :].bitcast(F32)]

        def gu_c(hc):
            return gu_half[hc // 2][:, 512 * (hc % 2):512 * (hc % 2 + 1)]
        xq5 = g5.enter_context(nc.sbuf_tensor("xq5", [P, D], BF16))
        xqT5 = g5.enter_context(nc.sbuf_tensor("xqT5", [P, D], BF16))
        sil = hn_tok[:, :].bitcast(F32)
        guqb = g5.enter_context(nc.sbuf_tensor("guqb", [P, 512], BF16))
        guqT = wqs_b[:, :].bitcast(BF16)
        ps_g = g5.enter_context(nc.psum_tensor("ps_g", [P, 512], F32))
        ps_u = g5.enter_context(nc.psum_tensor("ps_u", [P, 512], F32))
        ps_d = g5.enter_context(nc.psum_tensor("ps_d", [P, 512], F32))
        ps_tx5 = g5.enter_context(nc.psum_tensor("ps_tx5", [P, D], BF16))

        prev_ln = None
        prev_tx = None
        prev_gu = None
        prev_tq = None
        st_toks = []
        for g in range(NT):
            w0 = [prev_ln] if prev_ln else []
            t1 = V(lambda e, g=g: e.reduce_max(out=colb[:, RM:RM + 1], in_=streamT(g),
                                               axis=AX.X, apply_absolute_value=True),
                   waits=tuple(w0))
            t2 = V(lambda e: e.tensor_scalar(out=colb[:, MCL:MCL + 1], in0=colb[:, RM:RM + 1],
                                             scalar1=1e-5, scalar2=None, op0=ALU.max),
                   waits=(t1,))
            t3 = V(lambda e: e.reciprocal(out=colb[:, RC:RC + 1], in_=colb[:, MCL:MCL + 1]),
                   waits=(t2,))
            tsx = V(lambda e: e.tensor_scalar(out=colb[:, SX:SX + 1], in0=colb[:, RC:RC + 1],
                                              scalar1=127.0, scalar2=None, op0=ALU.mult),
                    waits=(t3,))
            Vn(lambda e: e.tensor_scalar(out=colb[:, DSC:DSC + 1], in0=colb[:, MCL:MCL + 1],
                                         scalar1=1.0 / 127.0, scalar2=dscwb["wg"][:, :],
                                         op0=ALU.mult, op1=ALU.mult))
            Vn(lambda e: e.tensor_scalar(out=colb[:, DSC2:DSC2 + 1], in0=colb[:, MCL:MCL + 1],
                                         scalar1=1.0 / 127.0, scalar2=dscwb["wu"][:, :],
                                         op0=ALU.mult, op1=ALU.mult))
            Vn(lambda e, g=g: e.tensor_scalar(out=gu_half[0].bitcast(I32), in0=streamT(g),
                                              scalar1=colb[:, SX:SX + 1], scalar2=None,
                                              op0=ALU.mult), waits=(tsx,))
            txq = V(lambda e: e.tensor_copy(out=xq5[:, :], in_=gu_half[0].bitcast(I32)))
            w0 = [txq] + ([prev_tx] if prev_tx else [])
            tb = None
            for dt in range(8):
                fn = (lambda e, d=dt: e.transpose(out=ps_tx5[:, P * d:P * (d + 1)],
                                                  in_=xq5[:, P * d:P * (d + 1)],
                                                  identity=identb[:, :]))
                if dt < 7:
                    Tn(fn, waits=tuple(w0) if dt == 0 else ())
                else:
                    tb = T(fn)
            prev_tx = V(lambda e: e.tensor_copy(out=xqT5[:, :], in_=ps_tx5[:, :]), waits=(tb,))
            mxt = None
            for hc in range(8):
                w0 = [prev_tx]
                if prev_gu is not None:
                    w0.append(prev_gu)
                tg = None
                for dt in range(8):
                    fn = (lambda e, d=dt, h=hc: e.matmul(
                        out=ps_g[:, :], lhsT=xqT5[:, P * d:P * (d + 1)],
                        rhs=wgq[:, HD * d + 512 * h:HD * d + 512 * (h + 1)],
                        start=(d == 0), stop=(d == 7)))
                    if dt < 7:
                        Tn(fn, waits=tuple(w0) if dt == 0 else ())
                    else:
                        tg = T(fn)
                tu = None
                for dt in range(8):
                    fn = (lambda e, d=dt, h=hc: e.matmul(
                        out=ps_u[:, :], lhsT=xqT5[:, P * d:P * (d + 1)],
                        rhs=wuq[:, HD * d + 512 * h:HD * d + 512 * (h + 1)],
                        start=(d == 0), stop=(d == 7)))
                    if dt < 7:
                        Tn(fn)
                    else:
                        tu = T(fn)
                sw = [tg] + ([prev_gu] if prev_gu else [])
                tsil = S(lambda e: e.activation(out=sil, in_=ps_g[:, :], func=AF.Silu,
                                                scale=colb[:, DSC:DSC + 1]),
                         waits=tuple(sw))
                prev_gu = V(lambda e, h=hc: e.scalar_tensor_tensor(
                    out=gu_c(h), in0=ps_u[:, :],
                    scalar=colb[:, DSC2:DSC2 + 1], in1=sil,
                    op0=ALU.mult, op1=ALU.mult), waits=(tu, tsil))
                mxt = V(lambda e, h=hc: e.reduce_max(
                    out=mx8[:, h:h + 1], in_=gu_c(h), axis=AX.X,
                    apply_absolute_value=True))
            Vn(lambda e: e.reduce_max(out=colb[:, RM:RM + 1], in_=mx8[:, :], axis=AX.X),
               waits=(mxt,))
            Vn(lambda e: e.tensor_scalar(out=colb[:, MCL:MCL + 1], in0=colb[:, RM:RM + 1],
                                         scalar1=1e-5, scalar2=None, op0=ALU.max))
            Vn(lambda e: e.reciprocal(out=colb[:, RC:RC + 1], in_=colb[:, MCL:MCL + 1]))
            tsx = V(lambda e: e.tensor_scalar(out=colb[:, SX:SX + 1], in0=colb[:, RC:RC + 1],
                                              scalar1=127.0, scalar2=None, op0=ALU.mult))
            Vn(lambda e: e.tensor_scalar(out=colb[:, DSC:DSC + 1], in0=colb[:, MCL:MCL + 1],
                                         scalar1=1.0 / 127.0, scalar2=dscwb["wd"][:, :],
                                         op0=ALU.mult, op1=ALU.mult))
            tq_ev = None
            for hc in range(8):
                Vn(lambda e, h=hc: e.tensor_scalar(
                    out=sil.bitcast(I32), in0=gu_c(h),
                    scalar1=colb[:, SX:SX + 1], scalar2=None, op0=ALU.mult),
                   waits=(tsx,) if hc == 0 else ())
                tqb = V(lambda e: e.tensor_copy(out=guqb[:, :], in_=sil.bitcast(I32)))
                w0 = [tqb] + ([tq_ev] if tq_ev else [prev_tx])
                tb = None
                for k in range(4):
                    fn = (lambda e, k=k: e.transpose(
                        out=ps_tx5[:, P * k:P * (k + 1)], in_=guqb[:, P * k:P * (k + 1)],
                        identity=identb[:, :]))
                    if k < 3:
                        Tn(fn, waits=tuple(w0) if k == 0 else ())
                    else:
                        tb = T(fn)
                tq_ev = V(lambda e, h=hc: e.tensor_copy(
                    out=guqT[:, 512 * h:512 * (h + 1)], in_=ps_tx5[:, 0:512]), waits=(tb,))
            prev_tx = tq_ev
            tstt = None
            for ec in range(2):
                w0 = [tq_ev] if ec == 0 else [tstt]
                tmm = None
                for ht in range(32):
                    fn = (lambda e, t=ht, x=ec: e.matmul(
                        out=ps_d[:, :], lhsT=guqT[:, P * t:P * (t + 1)],
                        rhs=wdq[:, D * t + 512 * x:D * t + 512 * (x + 1)],
                        start=(t == 0), stop=(t == 31)))
                    if ht < 31:
                        Tn(fn, waits=tuple(w0) if ht == 0 else ())
                    else:
                        tmm = T(fn)
                tstt = V(lambda e, g=g, x=ec: e.scalar_tensor_tensor(
                    out=scr1[:, 512 * x:512 * (x + 1)], in0=ps_d[:, :],
                    scalar=colb[:, DSC:DSC + 1],
                    in1=streamT(g)[:, 512 * x:512 * (x + 1)],
                    op0=ALU.mult, op1=ALU.add), waits=(tmm,))
            ta = ln_stats(lambda c0, c1: scr1[:, c0:c1], D, w0=(tstt,))
            _, tnm = rstd_nmu(ta)
            prev_ln = ln_apply(streamT(g), scr1[:, :], tnm)
            st_toks.append(DG(Y[P * g:P * (g + 1), :], streamT(g), waits=(prev_ln,)))
        q("gpsimd", None, None, (("qG", C["qG"]),))
        close_phase()
        g5.close()
        g5w.close()
        g4pre.close()
    return nc


def _emit_out(barrier, DG, C, cur, close_phase, stream, Y):
    barrier()
    for g in range(NT):
        DG(Y[P * g:P * (g + 1), :], stream[:, D * g:D * (g + 1)])
    cur["gpsimd"].append(((("qG", C["qG"]),), None, None))
    close_phase()


_BF = ml_dtypes.bfloat16


def _prep(inputs):
    f = np.float32
    c = np.ascontiguousarray
    shared = {
        "wqkvT": c(inputs["L0_attn_w_qkv"].T.astype(f)),
        "bqkvT": c(inputs["L0_attn_b_qkv"].astype(f).reshape(24, P).T),
        "woT": c(inputs["L0_attn_w_o"].T.astype(f)),
        "borow": c(inputs["L0_attn_b_o"].astype(f).reshape(1, D)),
        "bvrow": c(inputs["L0_attn_b_qkv"][2 * D:3 * D].astype(f).reshape(1, D)),
        "ln1gb": c(np.stack([inputs["L0_ffn_ln1_g"], inputs["L0_ffn_ln1_b"]]).astype(f)),
        "ln2gb": c(np.stack([inputs["L0_ffn_ln2_g"], inputs["L0_ffn_ln2_b"]]).astype(f)),
        "b1row": c(inputs["L0_ffn_b1"].astype(f).reshape(1, HD)),
        "b2row": c(inputs["L0_ffn_b2"].astype(f).reshape(1, D)),
        "w1T": c(inputs["L0_ffn_w1"].T.astype(f)),
        "w2T": c(inputs["L0_ffn_w2"].T.astype(f)),
        "wqkv1T": c(inputs["L1_attn_w_qkv"].T.astype(f)),
        "wout1T": c(inputs["L1_attn_w_out"].T.astype(f)),
        "wgT": c(inputs["L1_ffn_w_gate"].T.astype(f)),
        "wuT": c(inputs["L1_ffn_w_up"].T.astype(f)),
        "wdT": c(inputs["L1_ffn_w_down"].T.astype(f)),
        "identf": np.eye(P, dtype=f),
        "identb": np.eye(P).astype(_BF),
        "cmask": (np.arange(P)[None, :] >= np.arange(P)[:, None]).astype(_BF),
    }
    xs = np.asarray(inputs["L0_attn_x"], dtype=f)
    return shared, xs


def kernel(**inputs):
    shared, xs = _prep(inputs)
    nc = build()
    in_maps = [dict(shared, x=np.ascontiguousarray(xs[b])) for b in range(8)]
    res = run_bass_kernel_spmd(nc, in_maps, core_ids=list(range(8)))
    out = np.stack([r["y"] for r in res.results], axis=0)
    return out.astype(np.float32)


# version 04
# speedup vs baseline: 1.0891x; 1.0891x over previous
import contextlib
import numpy as np
import ml_dtypes

import concourse.bass as bass
import concourse.mybir as mybir
from concourse.bass_utils import run_bass_kernel_spmd

F32 = mybir.dt.float32
BF16 = mybir.dt.bfloat16
I32 = mybir.dt.int32
FP8 = mybir.dt.float8e4
F16 = mybir.dt.float16
AF = mybir.ActivationFunctionType
ALU = mybir.AluOpType
AX = mybir.AxisListType

P = 128
NT = 16
L = 2048
D = 1024
HD = 4096
EPS = 1e-5
ENGINES = ("sync", "gpsimd", "tensor", "vector", "scalar")
RSTD, NMU, RM, MCL, RC, SX, DSC, SXW, DSC2, NMUP = range(10)


def build(taps=(), n_phases=9):
    nc = bass.Bass("TRN2", target_bir_lowering=False)

    def din(name, shape, dt=F32):
        return nc.dram_tensor(name, list(shape), dt, kind="ExternalInput").ap()

    X = din("x", (L, D))
    WQKVT = din("wqkvT", (D, 3 * D))
    BQKVT = din("bqkvT", (P, 24))
    WOT = din("woT", (D, D))
    BOROW = din("borow", (1, D))
    BVROW = din("bvrow", (1, D))
    LN1GB = din("ln1gb", (2, D))
    LN2GB = din("ln2gb", (2, HD))
    B1ROW = din("b1row", (1, HD))
    B2ROW = din("b2row", (1, D))
    W1T = din("w1T", (D, HD))
    W2T = din("w2T", (HD, D))
    WQKV1T = din("wqkv1T", (D, 192))
    WOUT1T = din("wout1T", (64, D))
    WGT = din("wgT", (D, HD))
    WUT = din("wuT", (D, HD))
    WDT = din("wdT", (HD, D))
    IDF = din("identf", (P, P))
    IDB = din("identb", (P, P), BF16)
    CMASK = din("cmask", (P, P), BF16)

    Y = nc.dram_tensor("y", [L, D], F32, kind="ExternalOutput").ap()
    dbg = {}
    rawtap = {}
    for t in taps:
        if t.startswith("raw:"):
            _, nm, r, c, dt = t.split(":")
            rawtap[nm] = nc.dram_tensor("dbg_" + nm, [int(r), int(c)],
                                        {"bf16": BF16, "f32": F32}[dt],
                                        kind="ExternalOutput").ap()
        else:
            dbg[t] = nc.dram_tensor("dbg_" + t, [L, D], F32, kind="ExternalOutput").ap()

    C = {"qS": 0, "qG": 0, "tS": 0, "vS": 0, "sS": 0}
    sems = {}
    cur = {k: [] for k in ENGINES}

    def q(eng, fn, inc=None, waits=()):
        cur[eng].append((tuple(waits), fn, inc))

    def mk_runner(lst):
        def run(e):
            for waits, fn, inc in lst:
                for (s, v) in waits:
                    e.wait_ge(sems[s], v)
                if fn is not None:
                    inst = fn(e)
                    if inc is not None:
                        inst.then_inc(sems[inc[0]], inc[1])
        return run

    def close_phase():
        for eng in ENGINES:
            if cur[eng]:
                getattr(block, eng)(mk_runner(cur[eng]))
                cur[eng] = []

    def T(fn, waits=()):
        C["tS"] += 1
        q("tensor", fn, ("tS", 1), waits)
        return ("tS", C["tS"])

    def Tn(fn, waits=()):
        q("tensor", fn, None, waits)

    def V(fn, waits=()):
        C["vS"] += 1
        q("vector", fn, ("vS", 1), waits)
        return ("vS", C["vS"])

    def Vn(fn, waits=()):
        q("vector", fn, None, waits)

    def S(fn, waits=()):
        C["sS"] += 1
        q("scalar", fn, ("sS", 1), waits)
        return ("sS", C["sS"])

    def Sn(fn, waits=()):
        q("scalar", fn, None, waits)

    def DS(dst, src, waits=()):
        C["qS"] += 16
        q("sync", (lambda e, d=dst, s=src: e.dma_start(out=d, in_=s)), ("qS", 16), waits)
        return ("qS", C["qS"])

    def DG(dst, src, waits=()):
        C["qG"] += 16
        q("gpsimd", (lambda e, d=dst, s=src: e.dma_start(out=d, in_=s)), ("qG", 16), waits)
        return ("qG", C["qG"])

    def barrier():
        toks = [(k, C[k]) for k in C if C[k] > 0]
        for eng in ENGINES:
            q(eng, None, None, toks)

    es = contextlib.ExitStack()
    with es:
        for sn in ("qS", "qG", "tS", "vS", "sS"):
            sems[sn] = es.enter_context(nc.semaphore(sn))
        block = es.enter_context(nc.Block())

        stream = es.enter_context(nc.sbuf_tensor("stream", [P, NT * D], F32))
        identf = es.enter_context(nc.sbuf_tensor("sb_identf", [P, P], F32))
        identb = es.enter_context(nc.sbuf_tensor("sb_identb", [P, P], BF16))
        cmask = es.enter_context(nc.sbuf_tensor("sb_cmask", [P, P], BF16))
        bqkvT = es.enter_context(nc.sbuf_tensor("sb_bqkvT", [P, 24], F32))
        ones_f = es.enter_context(nc.sbuf_tensor("ones_f", [1, P], F32))
        ones_b = es.enter_context(nc.sbuf_tensor("ones_b", [1, P], BF16))
        colb = es.enter_context(nc.sbuf_tensor("colb", [P, 16], F32))
        aggrb = es.enter_context(nc.sbuf_tensor("aggrb", [P, 2], F32))
        bnb = es.enter_context(nc.sbuf_tensor("bnb", [P, 48], F32))
        mx8 = es.enter_context(nc.sbuf_tensor("mx8", [P, 8], F32))
        srcT = es.enter_context(nc.sbuf_tensor("srcT", [P, 2], F32))
        rowA = es.enter_context(nc.sbuf_tensor("rowA", [1, P], F16))
        rowB = es.enter_context(nc.sbuf_tensor("rowB", [2, P], F16))
        rowAug = es.enter_context(nc.sbuf_tensor("rowAug", [1, P], F16))
        scr1 = es.enter_context(nc.sbuf_tensor("scr1", [P, D], F32))
        sc2 = es.enter_context(nc.sbuf_tensor("sc2", [1, 2], F32))
        epscol = es.enter_context(nc.sbuf_tensor("epscol", [P, 1], F32))
        statlog = (es.enter_context(nc.sbuf_tensor("statlog", [P, 64], F32))
                   if "statlog" in rawtap else None)
        swb = {}
        dscwb = {}
        for wn in ("w1", "w2", "wg", "wu", "wd"):
            swb[wn] = es.enter_context(nc.sbuf_tensor("swb_" + wn, [P, 1], F32))
            dscwb[wn] = es.enter_context(nc.sbuf_tensor("dscw_" + wn, [P, 1], F32))

        streamT = lambda g: stream[:, D * g:D * (g + 1)]

        def rtap(name, sb):
            if name in rawtap:
                barrier()
                DG(rawtap[name], sb)
                barrier()

        def tap(name):
            if name in dbg:
                barrier()
                for g in range(NT):
                    DG(dbg[name][P * g:P * (g + 1), :], streamT(g))
                barrier()

        st_ln0 = {"tr": None, "rowB": None}

        def ln_stats(src_fn, width, w0=()):
            nch = width // 512
            tl = None
            for c in range(nch):
                tl = V(lambda e, c=c: e.bn_stats(out=bnb[:, 6 * c:6 * c + 6],
                                                 in_=src_fn(512 * c, 512 * (c + 1))),
                       waits=w0 if c == 0 else ())
            aw = [tl] + ([st_ln0["tr"]] if st_ln0["tr"] else [])
            return V(lambda e, n=nch: e.bn_aggr(out=aggrb[:, :], in_=bnb[:, 0:6 * n]),
                     waits=tuple(aw))

        def rstd_nmu(tok_aggr):
            tln = S(lambda e: e.activation(out=colb[:, 15:16], in_=aggrb[:, 1:2],
                                           func=AF.Ln, bias=epscol[:, :]),
                    waits=(tok_aggr,))
            tr = S(lambda e: e.activation(out=colb[:, RSTD:RSTD + 1],
                                          in_=colb[:, 15:16], func=AF.Exp,
                                          scale=-0.5), waits=(tln,))
            st_ln0["tr"] = tr
            Vn(lambda e: e.tensor_scalar(out=colb[:, NMU:NMU + 1], in0=aggrb[:, 0:1],
                                          scalar1=colb[:, RSTD:RSTD + 1], scalar2=-1.0,
                                          op0=ALU.mult, op1=ALU.mult), waits=(tr,))
            tn = V(lambda e: e.tensor_scalar(out=colb[:, NMUP:NMUP + 1], in0=aggrb[:, 0:1],
                                             scalar1=-1.0, scalar2=None, op0=ALU.mult))
            return tr, tn

        def ln_apply(dst, src, tok_nmu):
            return V(lambda e, d=dst, s=src: e.tensor_scalar(
                out=d, in0=s, scalar1=colb[:, NMUP:NMUP + 1], scalar2=colb[:, RSTD:RSTD + 1],
                op0=ALU.add, op1=ALU.mult), waits=(tok_nmu,))

        class WQ:

            def __init__(self, W, rows, name, dst, wqs, wqi, wpart, ps_q):
                self.W, self.name, self.dst = W, name, dst
                self.wqs, self.wqi, self.wpart, self.ps_q = wqs, wqi, wpart, ps_q
                ntile = rows // P
                self.cols = (8 * HD if rows == D else 32 * D) // ntile
                self.chunks = [(r, c) for r in range(ntile)
                               for c in range(self.cols // D)]
                self.i1 = 0
                self.i2 = 0
                self.prev = {}
                self.prev2 = {}
                self.tv2 = None

            def last_toks(self):
                d = dict(self.prev2 if self.prev2 else self.prev)
                return d

            def step1(self):
                if self.i1 < len(self.chunks):
                    idx = self.i1
                    r, c = self.chunks[idx]
                    half = slice(0, D) if idx % 2 == 0 else slice(D, 2 * D)
                    w0 = (self.prev[idx % 2],) if idx % 2 in self.prev else ()
                    td = DS(self.wqs[:, half],
                            self.W[P * r:P * (r + 1), D * c:D * (c + 1)], waits=w0)
                    self.prev[idx % 2] = V(
                        lambda e, h=half, i=idx, s=self: e.reduce_sum(
                            out=s.wpart[:, i:i + 1], in_=s.wqs[:, h], axis=AX.X,
                            apply_absolute_value=True), waits=(td,))
                    self.i1 += 1
                    return False
                tred = V(lambda e, s=self, n=len(self.chunks): e.reduce_sum(
                    out=colb[:, 10:11], in_=s.wpart[:, 0:n], axis=AX.X),
                    waits=tuple(self.prev.values()))
                tmm = T(lambda e, s=self: e.matmul(out=s.ps_q[0:1, :], lhsT=colb[:, 10:11],
                                                   rhs=srcT[:, 1:2], start=True, stop=True),
                        waits=(tred,))
                tms = V(lambda e, s=self: e.tensor_scalar(
                    out=sc2[0:1, 0:1], in0=s.ps_q[0:1, :], scalar1=1.0 / 4194304.0,
                    scalar2=1e-5, op0=ALU.mult, op1=ALU.max), waits=(tmm,))
                trc = V(lambda e: e.reciprocal(out=sc2[0:1, 1:2], in_=sc2[0:1, 0:1]),
                        waits=(tms,))
                tm1 = T(lambda e, s=self: e.matmul(out=s.ps_q[:, :], lhsT=ones_f[:, :],
                                                   rhs=sc2[0:1, 1:2], start=True, stop=True),
                        waits=(trc,))
                tv1 = V(lambda e, s=self: e.tensor_copy(out=swb[s.name][:, :],
                                                        in_=s.ps_q[:, :]), waits=(tm1,))
                tm2 = T(lambda e, s=self: e.matmul(out=s.ps_q[:, :], lhsT=ones_f[:, :],
                                                   rhs=sc2[0:1, 0:1], start=True, stop=True),
                        waits=(tv1,))
                self.tv2 = V(lambda e, s=self: e.tensor_copy(out=dscwb[s.name][:, :],
                                                             in_=s.ps_q[:, :]), waits=(tm2,))
                return True

            def step2(self):
                if self.i2 >= len(self.chunks):
                    return True
                idx = self.i2
                r, c = self.chunks[idx]
                half = slice(0, D) if idx % 2 == 0 else slice(D, 2 * D)
                w0 = [self.prev2[idx % 2]] if idx % 2 in self.prev2 else []
                if idx < 2 and self.tv2 is not None:
                    w0.append(self.tv2)
                td = DS(self.wqs[:, half],
                        self.W[P * r:P * (r + 1), D * c:D * (c + 1)], waits=tuple(w0))
                Vn(lambda e, h=half, s=self: e.tensor_scalar(
                    out=s.wqi[:, :], in0=s.wqs[:, h], scalar1=swb[s.name][:, :],
                    scalar2=None, op0=ALU.mult), waits=(td,))
                self.prev2[idx % 2] = V(
                    lambda e, s=self, r=r, c=c: e.tensor_scalar(
                        out=s.dst[:, s.cols * r + D * c:s.cols * r + D * (c + 1)],
                        in0=s.wqi[:, :], scalar1=-1.0, scalar2=1.0,
                        op0=ALU.max, op1=ALU.min))
                self.i2 += 1
                return self.i2 >= len(self.chunks)

        g_bo = contextlib.ExitStack()
        bo_b = g_bo.enter_context(nc.sbuf_tensor("bo_b", [P, D], F32))
        g_ht = contextlib.ExitStack()
        heads_tok = g_ht.enter_context(nc.sbuf_tensor("heads_tok", [P, NT * D], BF16))
        g_x0t = contextlib.ExitStack()
        X0T = g_x0t.enter_context(nc.sbuf_tensor("X0T", [P, 8 * L], BF16))
        X0Tr = X0T[:, :].rearrange("p (t i) -> p t i", i=L)

        g0 = contextlib.ExitStack()
        staging = g0.enter_context(nc.sbuf_tensor("staging", [P, HD], F32))
        brow2 = g0.enter_context(nc.sbuf_tensor("brow2", [1, D], F32))
        ps_bc = g0.enter_context(nc.psum_tensor("ps_bc", [P, D], F32))
        ps_tx0 = g0.enter_context(nc.psum_tensor("ps_tx0", [P, 512], F32))
        ps_tx1 = g0.enter_context(nc.psum_tensor("ps_tx1", [P, 512], F32))

        ld_x = [DS(streamT(g), X[P * g:P * (g + 1), :]) for g in range(NT)]
        t_idf = DS(identf[:, :], IDF)
        DS(identb[:, :], IDB)
        DS(cmask[:, :], CMASK)
        DS(bqkvT[:, :], BQKVT)
        t_bo = DS(brow2[0:1, 0:D], BOROW)
        Vn(lambda e: e.memset(ones_f[:, :], 1.0))
        Vn(lambda e: e.memset(ones_b[:, :], 1.0))
        Vn(lambda e: e.memset(srcT[:, 1:2], 1.0))
        Vn(lambda e: e.memset(epscol[:, :], EPS))
        t_cst = V(lambda e: e.memset(srcT[:, 0:1], 0.0))

        Tn(lambda e: e.matmul(out=ps_bc[:, 0:512], lhsT=ones_f[:, :], rhs=brow2[0:1, 0:512],
                              start=True, stop=True), waits=(t_bo, t_cst))
        tk = T(lambda e: e.matmul(out=ps_bc[:, 512:1024], lhsT=ones_f[:, :],
                                  rhs=brow2[0:1, 512:1024], start=True, stop=True))
        V(lambda e: e.tensor_copy(out=bo_b[:, :], in_=ps_bc[:, :]), waits=(tk,))

        ev_tok = {0: None, 1: None}
        for g in range(NT):
            for bb in range(2):
                bank = [ps_tx0, ps_tx1][bb]
                w0 = [ld_x[g], t_idf]
                if ev_tok[bb] is not None:
                    w0.append(ev_tok[bb])
                tb = None
                for k in range(4):
                    dt = 4 * bb + k
                    fn = (lambda e, g=g, d=dt, b=bank, k=k: e.transpose(
                        out=b[:, P * k:P * (k + 1)],
                        in_=stream[:, D * g + P * d:D * g + P * (d + 1)],
                        identity=identf[:, :]))
                    if k < 3:
                        Tn(fn, waits=tuple(w0) if k == 0 else ())
                    else:
                        tb = T(fn)
                ev_tok[bb] = V(lambda e, g=g, bb=bb, b=bank: e.tensor_copy(
                    out=X0Tr[:, 4 * bb:4 * bb + 4, P * g:P * (g + 1)],
                    in_=b[:, :].rearrange("p (t c) -> p t c", c=P)), waits=(tb,))
        close_phase()
        g0.close()

        barrier()
        g_va = contextlib.ExitStack()
        v_aug = g_va.enter_context(nc.sbuf_tensor("v_aug", [P, NT * 1040], BF16))
        v_augR = v_aug[:, :].rearrange("p (j h c) -> p j h c", h=NT, c=65)

        g_bv = contextlib.ExitStack()
        bv_b = g_bv.enter_context(nc.sbuf_tensor("bv_b", [P, D], F32))
        g1 = contextlib.ExitStack()
        wst = g1.enter_context(nc.sbuf_tensor("wst", [P, 2 * D], F32))
        wvb = g1.enter_context(nc.sbuf_tensor("wvb", [P, 8 * D], BF16))
        bvrow = g1.enter_context(nc.sbuf_tensor("sb_bvrow", [1, D], F32))
        ps_a = g1.enter_context(nc.psum_tensor("ps_a", [P, 512], F32))
        ps_b = g1.enter_context(nc.psum_tensor("ps_b", [P, 512], F32))
        t_bv = DS(bvrow[0:1, :], BVROW)
        Tn(lambda e: e.matmul(out=ps_a[:, :], lhsT=ones_f[:, :], rhs=bvrow[0:1, 0:512],
                              start=True, stop=True), waits=(t_bv,))
        tkv = T(lambda e: e.matmul(out=ps_b[:, :], lhsT=ones_f[:, :], rhs=bvrow[0:1, 512:D],
                                   start=True, stop=True))
        t_bvb = V(lambda e: e.tensor_copy(out=bv_b[:, 0:512], in_=ps_a[:, :]), waits=(tkv,))
        t_bvb = V(lambda e: e.tensor_copy(out=bv_b[:, 512:D], in_=ps_b[:, :]))

        wv_cast = []
        for dt in range(8):
            half = slice(0, D) if dt % 2 == 0 else slice(D, 2 * D)
            wwait = (wv_cast[dt - 2],) if dt >= 2 else ()
            td = DS(wst[:, half], WQKVT[P * dt:P * (dt + 1), 2 * D:3 * D], waits=wwait)
            wv_cast.append(V(lambda e, d=dt, h=half: e.tensor_copy(
                out=wvb[:, D * d:D * (d + 1)], in_=wst[:, h]), waits=(td,)))
        t_vones = V(lambda e: e.memset(v_augR[:, :, :, 64:65], 1.0))
        evv = {0: None, 1: None}
        for g in range(NT):
            for cc in range(2):
                bid = (2 * g + cc) % 2
                bank = [ps_a, ps_b][bid]
                w0 = [wv_cast[-1], t_vones, t_bvb] if evv[bid] is None else [evv[bid]]
                tmm = None
                for dt in range(8):
                    fn = (lambda e, g=g, c=cc, d=dt, b=bank: e.matmul(
                        out=b[:, :], lhsT=X0Tr[:, d, P * g:P * (g + 1)],
                        rhs=wvb[:, D * d + 512 * c:D * d + 512 * (c + 1)],
                        start=(d == 0), stop=(d == 7)))
                    if dt < 7:
                        Tn(fn, waits=tuple(w0) if dt == 0 else ())
                    else:
                        tmm = T(fn)
                evv[bid] = V(lambda e, g=g, c=cc, b=bank: e.tensor_tensor(
                    out=v_augR[:, g, 8 * c:8 * (c + 1), 0:64],
                    in0=b[:, :].rearrange("p (h c) -> p h c", c=64),
                    in1=bv_b[:, 512 * c:512 * (c + 1)].rearrange("p (h c) -> p h c", c=64),
                    op=ALU.add), waits=(tmm,))
        close_phase()
        g1.close()
        g_bv.close()

        barrier()
        g2 = contextlib.ExitStack()
        qTp = g2.enter_context(nc.sbuf_tensor("qTp", [P, 2 * L], BF16))
        kTp = g2.enter_context(nc.sbuf_tensor("kTp", [P, 2 * L], BF16))
        wst2 = g2.enter_context(nc.sbuf_tensor("wst2", [P, D], F32))
        wbb2 = g2.enter_context(nc.sbuf_tensor("wbb2", [P, 2 * D], BF16))
        E0 = g2.enter_context(nc.sbuf_tensor("E0", [P, 1024], BF16))
        E1 = g2.enter_context(nc.sbuf_tensor("E1", [P, 1024], BF16))
        rdb4 = g2.enter_context(nc.sbuf_tensor("rdb4", [P, 4], F32))
        wqs_c = g2.enter_context(nc.sbuf_tensor("wqs_c", [P, 2 * D], F32))
        wpart_c = bnb
        ps_s = [g2.enter_context(nc.psum_tensor("ps_s%d" % i, [P, 512], F32)) for i in range(4)]
        ps_av = [g2.enter_context(nc.psum_tensor("ps_av%d" % i, [P, 512], F32)) for i in range(2)]
        ps_qc = g2.enter_context(nc.psum_tensor("ps_qc", [P, 1], F32))
        ps_pj = g2.enter_context(nc.psum_tensor("ps_pj", [P, 512], F32))
        wq1o = WQ(W1T, D, "w1", None, wqs_c, None, wpart_c, ps_qc)
        wq2o = WQ(W2T, HD, "w2", None, wqs_c, None, wpart_c, ps_qc)
        wq_listA = [wq1o, wq2o]

        def wqA_tick(n):
            for _ in range(n):
                if not wq_listA:
                    return
                if wq_listA[0].step1():
                    done = wq_listA.pop(0)
                    if wq_listA:
                        wq_listA[0].prev = dict(done.prev)

        sbank_tok = {}
        ehalf_tok = {}
        avfree = {}
        t_cast2 = None
        t_projmm = None
        t_pairlast = {}
        t_proj = {}
        Ebuf = [E0, E1]

        def emit_proj_dma(p):
            nonlocal t_cast2, t_projmm
            dw = (t_cast2,) if t_cast2 else ()
            tdq = DS(wst2[:, :].rearrange("p (t c) -> p t c", c=P),
                     WQKVT[:, P * p:P * (p + 1)].rearrange("(t p) c -> p t c", p=P),
                     waits=dw)
            cwq = [tdq]
            if t_projmm is not None:
                cwq.append(t_projmm)
            tcq = V(lambda e: e.tensor_copy(out=wbb2[:, 0:D], in_=wst2[:, :]),
                    waits=tuple(cwq))
            tdk = DS(wst2[:, :].rearrange("p (t c) -> p t c", c=P),
                     WQKVT[:, D + P * p:D + P * (p + 1)].rearrange("(t p) c -> p t c", p=P),
                     waits=(tcq,))
            t_cast2 = V(lambda e: e.tensor_copy(out=wbb2[:, D:2 * D], in_=wst2[:, :]),
                        waits=(tdk,))

        def emit_proj_group(p, grp):
            nonlocal t_projmm
            qk, ic = grp // 4, grp % 4
            par_p = p % 2
            dstbuf = qTp if qk == 0 else kTp
            bcol = p if qk == 0 else 8 + p
            w0 = [t_cast2]
            if p >= 2 and par_p in t_pairlast:
                w0.append(t_pairlast[par_p])
            if t_proj.get((p, grp - 1)) is not None:
                w0 = [t_proj[(p, grp - 1)]]
            tmm = None
            for dt in range(8):
                fn = (lambda e, qk=qk, d=dt, i=ic: e.matmul(
                    out=ps_pj[:, :], lhsT=wbb2[:, D * qk + P * d:D * qk + P * (d + 1)],
                    rhs=X0Tr[:, d, 512 * i:512 * (i + 1)],
                    start=(d == 0), stop=(d == 7)))
                if dt < 7:
                    Tn(fn, waits=tuple(w0) if dt == 0 else ())
                else:
                    tmm = T(fn)
            t_projmm = tmm
            if qk == 0:
                tok = V(lambda e, d=dstbuf, i=ic, bc=bcol, pp=par_p: e.tensor_scalar(
                    out=d[:, L * pp + 512 * i:L * pp + 512 * (i + 1)],
                    in0=ps_pj[:, :], scalar1=bqkvT[:, bc:bc + 1],
                    scalar2=1.0 / 32.0, op0=ALU.add, op1=ALU.mult), waits=(tmm,))
            else:
                tok = V(lambda e, d=dstbuf, i=ic, bc=bcol, pp=par_p: e.tensor_scalar(
                    out=d[:, L * pp + 512 * i:L * pp + 512 * (i + 1)],
                    in0=ps_pj[:, :], scalar1=bqkvT[:, bc:bc + 1],
                    scalar2=None, op0=ALU.add), waits=(tmm,))
            t_proj[(p, grp)] = tok

        emit_proj_dma(0)
        for grp in range(8):
            emit_proj_group(0, grp)
        emit_proj_dma(1)

        for p in range(8):
            par_p = p % 2
            tproj_done = t_proj[(p, 7)]
            for ic in range(4):
                avjt = {}
                tok_sc = {}

                def emit_scores(jt, ic=None, par_p=par_p, tproj_done=tproj_done):
                    par = jt % 2
                    for h01 in range(2):
                        bank = ps_s[2 * h01 + par]
                        rb = 64 * h01
                        w0 = [tproj_done]
                        if (h01, par) in sbank_tok:
                            w0.append(sbank_tok[(h01, par)])
                        tok_sc[(h01, jt)] = T(
                            lambda e, pp=par_p, h=h01, j=jt, i=ic, b=bank, r=rb:
                            e.matmul(out=b[:, :],
                                     lhsT=kTp[r:r + 64, L * pp + P * j:L * pp + P * (j + 1)],
                                     rhs=qTp[r:r + 64, L * pp + 512 * i:L * pp + 512 * (i + 1)],
                                     start=True, stop=True), waits=tuple(w0))

                emit_scores(0, ic)
                for jt in range(NT):
                    par = jt % 2
                    if jt + 1 < NT:
                        emit_scores(jt + 1, ic)
                    if ic == 3 and jt == NT - 1:
                        t_pairlast[par_p] = tok_sc[(1, jt)]
                    for h01 in range(2):
                        bank = ps_s[2 * h01 + par]
                        w0 = [tok_sc[(h01, jt)]]
                        if (h01, par) in ehalf_tok:
                            w0.append(ehalf_tok[(h01, par)])
                        sbank_tok[(h01, par)] = S(lambda e, h=h01, j=jt, b=bank: e.activation(
                            out=Ebuf[h][:, 512 * (j % 2):512 * (j % 2 + 1)], in_=b[:, :],
                            func=AF.Exp), waits=tuple(w0))
                    for h01 in range(2):
                        hglob = 2 * p + h01
                        w0 = [sbank_tok[(h01, par)]]
                        if jt == 0 and h01 in avfree:
                            w0.append(avfree[h01])
                        tlast = None
                        for it in range(4):
                            fn = (lambda e, h=h01, hg=hglob, j=jt, t=it: e.matmul(
                                out=ps_av[h][:, 65 * t:65 * (t + 1)],
                                lhsT=Ebuf[h][:, 512 * (j % 2) + P * t:512 * (j % 2) + P * (t + 1)],
                                rhs=v_augR[:, j, hg, :],
                                start=(j == 0 and t == 0), stop=(j == NT - 1 and t == 3),
                                skip_group_check=True))
                            if it < 3:
                                Tn(fn, waits=tuple(w0) if it == 0 else ())
                            else:
                                tlast = T(fn)
                        ehalf_tok[(h01, par)] = tlast
                        avjt[(h01, jt)] = tlast
                for h01 in range(2):
                    hglob = 2 * p + h01
                    avR = ps_av[h01][:, 0:260].rearrange("p (i c) -> p i c", c=65)
                    trd = V(lambda e, h=h01, a=avR: e.reciprocal(out=rdb4[:, :],
                                                                 in_=a[:, :, 64:65]),
                            waits=(avjt[(h01, NT - 1)],))
                    tn = None
                    for it in range(4):
                        g = 4 * ic + it
                        fn = (lambda e, h=h01, hg=hglob, a=avR, t=it, g=g: e.tensor_scalar(
                            out=heads_tok[:, D * g + 64 * hg:D * g + 64 * (hg + 1)],
                            in0=a[:, t, 0:64], scalar1=rdb4[:, t:t + 1], scalar2=None,
                            op0=ALU.mult))
                        if it < 3:
                            Vn(fn, waits=(trd,) if it == 0 else ())
                        else:
                            tn = V(fn)
                    avfree[h01] = tn
                wqA_tick(3)
                if p + 1 < 8:
                    emit_proj_group(p + 1, 2 * ic)
                    emit_proj_group(p + 1, 2 * ic + 1)
            if p + 2 < 8:
                emit_proj_dma(p + 2)
        wqA_tick(300)
        rtap("heads_tok", heads_tok[:, :])
        rtap("qTp", qTp[:, :])
        rtap("kTp", kTp[:, :])
        close_phase()
        g2.close()
        g_va.close()
        g_x0t.close()

        barrier()
        g2b = contextlib.ExitStack()
        headsT = g2b.enter_context(nc.sbuf_tensor("headsT", [P, 8 * L], BF16))
        headsTr = headsT[:, :].rearrange("p (c i) -> p c i", i=L)
        wo_b = g2b.enter_context(nc.sbuf_tensor("wo_b", [P, 8 * D], BF16))
        wst3 = g2b.enter_context(nc.sbuf_tensor("wst3", [P, 2 * D], F32))
        ps_txb = g2b.enter_context(nc.psum_tensor("ps_txb", [P, D], BF16))
        ps_o = g2b.enter_context(nc.psum_tensor("ps_o", [P, D], F32))

        wo_cast = []
        for ct in range(8):
            half = slice(0, D) if ct % 2 == 0 else slice(D, 2 * D)
            wwait = (wo_cast[ct - 2],) if ct >= 2 else ()
            td = DS(wst3[:, half], WOT[P * ct:P * (ct + 1), :], waits=wwait)
            wo_cast.append(V(lambda e, c=ct, h=half: e.tensor_copy(
                out=wo_b[:, D * c:D * (c + 1)], in_=wst3[:, h]), waits=(td,)))

        tx_ev = None
        po_ev = None
        for g in range(NT):
            w0 = [] if tx_ev is None else [tx_ev]
            tb = None
            for ct in range(8):
                fn = (lambda e, g=g, c=ct: e.transpose(
                    out=ps_txb[:, P * c:P * (c + 1)],
                    in_=heads_tok[:, D * g + P * c:D * g + P * (c + 1)],
                    identity=identb[:, :]))
                if ct < 7:
                    Tn(fn, waits=tuple(w0) if ct == 0 else ())
                else:
                    tb = T(fn)
            tx_ev = V(lambda e, g=g: e.tensor_copy(
                out=headsTr[:, 0:8, P * g:P * (g + 1)],
                in_=ps_txb[:, :].rearrange("p (c i) -> p c i", i=P)), waits=(tb,))
        for g in range(NT):
            tmm = None
            for ec in range(2):
                w0 = [tx_ev, wo_cast[-1]] if (g == 0 and ec == 0) else []
                if po_ev is not None and ec == 0:
                    w0 = [po_ev]
                for ct in range(8):
                    fn = (lambda e, g=g, c=ct, x=ec: e.matmul(
                        out=ps_o[:, 512 * x:512 * (x + 1)],
                        lhsT=headsTr[:, c, P * g:P * (g + 1)],
                        rhs=wo_b[:, D * c + 512 * x:D * c + 512 * (x + 1)],
                        start=(c == 0), stop=(c == 7)))
                    if ct < 7:
                        Tn(fn, waits=tuple(w0) if ct == 0 else ())
                    else:
                        tmm = T(fn)
            Vn(lambda e, g=g: e.scalar_tensor_tensor(
                out=scr1[:, :], in0=streamT(g), scalar=2.0, in1=bo_b[:, :],
                op0=ALU.mult, op1=ALU.add), waits=(tmm,))
            tadd = V(lambda e: e.tensor_tensor(out=scr1[:, :], in0=scr1[:, :],
                                               in1=ps_o[:, :], op=ALU.add))
            po_ev = tadd
            ta = ln_stats(lambda c0, c1: scr1[:, c0:c1], D)
            _, tnm = rstd_nmu(ta)
            if statlog is not None:
                Vn(lambda e, g=g: e.tensor_copy(out=statlog[:, 4 * g:4 * g + 2],
                                                in_=aggrb[:, :]), waits=(tnm,))
                Vn(lambda e, g=g: e.tensor_copy(out=statlog[:, 4 * g + 2:4 * g + 3],
                                                in_=colb[:, RSTD:RSTD + 1]))
                Vn(lambda e, g=g: e.tensor_copy(out=statlog[:, 4 * g + 3:4 * g + 4],
                                                in_=colb[:, NMUP:NMUP + 1]))
            ln_apply(streamT(g), scr1[:, :], tnm)
        if statlog is not None:
            rtap("statlog", statlog[:, :])
        close_phase()
        g2b.close()
        g_ht.close()
        g_bo.close()
        tap("l0attn")
        if n_phases <= 2:
            _emit_out(barrier, DG, C, cur, close_phase, stream, Y)
            return nc

        barrier()
        g_l0c = contextlib.ExitStack()
        cln1 = g_l0c.enter_context(nc.sbuf_tensor("cln1", [2, D], F16))
        cln2 = g_l0c.enter_context(nc.sbuf_tensor("cln2", [2, HD], F16))
        cb1 = g_l0c.enter_context(nc.sbuf_tensor("cb1", [1, HD], F16))
        cb2 = g_l0c.enter_context(nc.sbuf_tensor("cb2", [1, D], F16))
        gst = contextlib.ExitStack()
        stg2 = gst.enter_context(nc.sbuf_tensor("stg2", [2, HD], F32))
        t_r1 = DS(stg2[0:2, 0:D], LN1GB)
        tc1 = V(lambda e: e.tensor_copy(out=cln1[:, :], in_=stg2[0:2, 0:D]), waits=(t_r1,))
        t_r2 = DS(stg2[0:2, 0:HD], LN2GB, waits=(tc1,))
        tc2 = V(lambda e: e.tensor_copy(out=cln2[:, :], in_=stg2[0:2, 0:HD]), waits=(t_r2,))
        t_r3 = DS(stg2[0:1, 0:HD], B1ROW, waits=(tc2,))
        tc3 = V(lambda e: e.tensor_copy(out=cb1[:, :], in_=stg2[0:1, 0:HD]), waits=(t_r3,))
        t_r4 = DS(stg2[0:1, 0:D], B2ROW, waits=(tc3,))
        Vn(lambda e: e.tensor_copy(out=cb2[:, :], in_=stg2[0:1, 0:D]), waits=(t_r4,))
        close_phase()
        gst.close()
        g_w12 = contextlib.ExitStack()
        w1q = g_w12.enter_context(nc.sbuf_tensor("w1q", [P, 8 * HD], FP8))
        w2q = g_w12.enter_context(nc.sbuf_tensor("w2q", [P, 32 * D], FP8))


        gqs = contextlib.ExitStack()
        wqs_a = gqs.enter_context(nc.sbuf_tensor("wqs_a", [P, 2 * D], F32))
        wqi_a = gqs.enter_context(nc.sbuf_tensor("wqi_a", [P, D], I32))
        prev_rounds = {}
        for o, d in ((wq1o, w1q), (wq2o, w2q)):
            o.wqs = wqs_a
            o.wqi = wqi_a
            o.dst = d
            o.prev2 = dict(prev_rounds)
            while not o.step2():
                pass
            prev_rounds = dict(o.prev2)
        barrier()
        close_phase()
        gqs.close()

        g_x1t = contextlib.ExitStack()
        g3 = contextlib.ExitStack()
        gbuf = g3.enter_context(nc.sbuf_tensor("gbuf", [P, HD], F32))
        xq = g3.enter_context(nc.sbuf_tensor("xq", [P, D], BF16))
        xqT = g3.enter_context(nc.sbuf_tensor("xqT", [P, D], BF16))
        h2c = g3.enter_context(nc.sbuf_tensor("h2c", [P, 512], F32))
        h2qb = g3.enter_context(nc.sbuf_tensor("h2qb", [P, 512], BF16))
        h2qT = g3.enter_context(nc.sbuf_tensor("h2qT", [P, HD], BF16))
        ps_r = g3.enter_context(nc.psum_tensor("ps_r", [2, P], F32))
        ps_A = g3.enter_context(nc.psum_tensor("ps_A", [P, 512], F32))
        ps_B = g3.enter_context(nc.psum_tensor("ps_B", [P, 512], F32))
        ps_1a = g3.enter_context(nc.psum_tensor("ps_1a", [P, 512], F32))
        ps_1b = g3.enter_context(nc.psum_tensor("ps_1b", [P, 512], F32))
        ps_2 = g3.enter_context(nc.psum_tensor("ps_2", [P, 512], F32))
        ps_txq = g3.enter_context(nc.psum_tensor("ps_txq", [P, D], BF16))

        def mkrow(dst, src_col, k, w0):
            tt = T(lambda e, s=src_col, k=k: e.transpose(out=ps_r[0:k, :], in_=s,
                                                         identity=identf[:, :]),
                   waits=tuple(w0))
            return V(lambda e, d=dst, k=k: e.tensor_copy(out=d, in_=ps_r[0:k, :]), waits=(tt,))

        prev_gelu = {}
        prev_ap = None
        prev_w2 = None
        prev_tx = None
        for g in range(NT):
            ta = ln_stats(lambda c0, c1, g=g: streamT(g)[:, c0:c1], D,
                          w0=(prev_w2,) if prev_w2 else ())
            trs, tnm = rstd_nmu(ta)
            trowA = mkrow(rowA[:, :], colb[:, RSTD:RSTD + 1], 1, [trs] +
                          ([prev_ap] if prev_ap else []))
            sw0 = [tnm] + ([st_ln0["rowB"]] if st_ln0["rowB"] else [])
            tsr = V(lambda e: e.tensor_copy(out=srcT[:, 0:1], in_=colb[:, NMU:NMU + 1]),
                    waits=tuple(sw0))
            trowB = mkrow(rowB[:, :], srcT[:, :], 2, [tsr, trowA])
            st_ln0["rowB"] = trowB
            tap_tok = None
            for dc in range(2):
                tA = T(lambda e, d=dc: e.matmul(out=ps_A[:, :], lhsT=rowA[:, :],
                                                rhs=cln1[0:1, 512 * d:512 * (d + 1)],
                                                start=True, stop=True),
                       waits=(trowA, trowB) if dc == 0 else (tap_tok,))
                tB = T(lambda e, d=dc: e.matmul(out=ps_B[:, :], lhsT=rowB[:, :],
                                                rhs=cln1[0:2, 512 * d:512 * (d + 1)],
                                                start=True, stop=True))
                Vn(lambda e, g=g, d=dc: e.tensor_tensor(
                    out=gbuf[:, 512 * d:512 * (d + 1)],
                    in0=streamT(g)[:, 512 * d:512 * (d + 1)], in1=ps_A[:, :], op=ALU.mult),
                   waits=(tB,))
                tap_tok = V(lambda e, d=dc: e.tensor_tensor(
                    out=gbuf[:, 512 * d:512 * (d + 1)], in0=gbuf[:, 512 * d:512 * (d + 1)],
                    in1=ps_B[:, :], op=ALU.add))
            prev_ap = tap_tok
            t1 = V(lambda e: e.reduce_max(out=colb[:, RM:RM + 1], in_=gbuf[:, 0:D],
                                          axis=AX.X, apply_absolute_value=True),
                   waits=(tap_tok,))
            t2 = V(lambda e: e.tensor_scalar(out=colb[:, MCL:MCL + 1], in0=colb[:, RM:RM + 1],
                                             scalar1=1e-5, scalar2=None, op0=ALU.max),
                   waits=(t1,))
            t3 = V(lambda e: e.reciprocal(out=colb[:, RC:RC + 1], in_=colb[:, MCL:MCL + 1]),
                   waits=(t2,))
            tsx = V(lambda e: e.tensor_scalar(out=colb[:, SX:SX + 1], in0=colb[:, RC:RC + 1],
                                              scalar1=127.0, scalar2=None, op0=ALU.mult),
                    waits=(t3,))
            Vn(lambda e: e.tensor_scalar(out=colb[:, DSC:DSC + 1], in0=colb[:, MCL:MCL + 1],
                                         scalar1=1.0 / 127.0, scalar2=dscwb["w1"][:, :],
                                         op0=ALU.mult, op1=ALU.mult))
            tsxw = V(lambda e: e.tensor_scalar(out=colb[:, SXW:SXW + 1], in0=colb[:, SX:SX + 1],
                                               scalar1=swb["w1"][:, :], scalar2=None,
                                               op0=ALU.mult))
            trAug = mkrow(rowAug[:, :], colb[:, SXW:SXW + 1], 1, [tsxw, trowB])
            Vn(lambda e: e.tensor_scalar(out=gbuf[:, D:2 * D].bitcast(I32), in0=gbuf[:, 0:D],
                                         scalar1=colb[:, SX:SX + 1], scalar2=None,
                                         op0=ALU.mult), waits=(tsx,))
            txq = V(lambda e: e.tensor_copy(out=xq[:, :], in_=gbuf[:, D:2 * D].bitcast(I32)))
            w0 = [txq] + ([prev_tx] if prev_tx else [])
            tb = None
            for dt in range(8):
                fn = (lambda e, d=dt: e.transpose(out=ps_txq[:, P * d:P * (d + 1)],
                                                  in_=xq[:, P * d:P * (d + 1)],
                                                  identity=identb[:, :]))
                if dt < 7:
                    Tn(fn, waits=tuple(w0) if dt == 0 else ())
                else:
                    tb = T(fn)
            prev_tx = V(lambda e: e.tensor_copy(out=xqT[:, :], in_=ps_txq[:, :]), waits=(tb,))
            for hc in range(8):
                bank = [ps_1a, ps_1b][hc % 2]
                w0 = [prev_tx]
                if (hc % 2) in prev_gelu:
                    w0.append(prev_gelu[hc % 2])
                tmm = None
                for dt in range(8):
                    fn = (lambda e, d=dt, h=hc, b=bank: e.matmul(
                        out=b[:, :], lhsT=xqT[:, P * d:P * (d + 1)],
                        rhs=w1q[:, HD * d + 512 * h:HD * d + 512 * (h + 1)],
                        start=(d == 0), stop=False))
                    Tn(fn, waits=tuple(w0) if dt == 0 else ())
                tmm = T(lambda e, h=hc, b=bank: e.matmul(
                    out=b[:, :], lhsT=rowAug[:, :], rhs=cb1[0:1, 512 * h:512 * (h + 1)],
                    start=False, stop=True), waits=(trAug,))
                tg = S(lambda e, h=hc, b=bank: e.activation(
                    out=gbuf[:, 512 * h:512 * (h + 1)], in_=b[:, :], func=AF.Gelu,
                    scale=colb[:, DSC:DSC + 1]), waits=(tmm,))
                prev_gelu[hc % 2] = V(lambda e, h=hc: e.bn_stats(
                    out=bnb[:, 6 * h:6 * h + 6], in_=gbuf[:, 512 * h:512 * (h + 1)]),
                    waits=(tg,))
            ta2 = V(lambda e: e.bn_aggr(out=aggrb[:, :], in_=bnb[:, 0:48]),
                    waits=(prev_gelu[0], prev_gelu[1]))
            trs2, tnm2 = rstd_nmu(ta2)
            trowA = mkrow(rowA[:, :], colb[:, RSTD:RSTD + 1], 1, [trs2, trAug])
            sw0 = [tnm2] + ([st_ln0["rowB"]] if st_ln0["rowB"] else [])
            tsr = V(lambda e: e.tensor_copy(out=srcT[:, 0:1], in_=colb[:, NMU:NMU + 1]),
                    waits=tuple(sw0))
            trowB = mkrow(rowB[:, :], srcT[:, :], 2, [tsr, trowA])
            st_ln0["rowB"] = trowB
            tmax = None
            tap2 = None
            for hc in range(8):
                tA = T(lambda e, h=hc: e.matmul(out=ps_A[:, :], lhsT=rowA[:, :],
                                                rhs=cln2[0:1, 512 * h:512 * (h + 1)],
                                                start=True, stop=True),
                       waits=(trowA, trowB) if hc == 0 else (tap2,))
                tB = T(lambda e, h=hc: e.matmul(out=ps_B[:, :], lhsT=rowB[:, :],
                                                rhs=cln2[0:2, 512 * h:512 * (h + 1)],
                                                start=True, stop=True))
                Vn(lambda e, h=hc: e.tensor_tensor(out=h2c[:, :],
                                                   in0=gbuf[:, 512 * h:512 * (h + 1)],
                                                   in1=ps_A[:, :], op=ALU.mult), waits=(tB,))
                tap2 = V(lambda e, h=hc: e.tensor_tensor(
                    out=gbuf[:, 512 * h:512 * (h + 1)], in0=h2c[:, :], in1=ps_B[:, :],
                    op=ALU.add))
                tmax = V(lambda e, h=hc: e.reduce_max(
                    out=mx8[:, h:h + 1], in_=gbuf[:, 512 * h:512 * (h + 1)], axis=AX.X,
                    apply_absolute_value=True))
            t1 = V(lambda e: e.reduce_max(out=colb[:, RM:RM + 1], in_=mx8[:, :], axis=AX.X),
                   waits=(tmax,))
            t2 = V(lambda e: e.tensor_scalar(out=colb[:, MCL:MCL + 1], in0=colb[:, RM:RM + 1],
                                             scalar1=1e-5, scalar2=None, op0=ALU.max),
                   waits=(t1,))
            t3 = V(lambda e: e.reciprocal(out=colb[:, RC:RC + 1], in_=colb[:, MCL:MCL + 1]),
                   waits=(t2,))
            tsx = V(lambda e: e.tensor_scalar(out=colb[:, SX:SX + 1], in0=colb[:, RC:RC + 1],
                                              scalar1=127.0, scalar2=None, op0=ALU.mult),
                    waits=(t3,))
            Vn(lambda e: e.tensor_scalar(out=colb[:, DSC2:DSC2 + 1], in0=colb[:, MCL:MCL + 1],
                                         scalar1=1.0 / 127.0, scalar2=dscwb["w2"][:, :],
                                         op0=ALU.mult, op1=ALU.mult))
            tsxw = V(lambda e: e.tensor_scalar(out=colb[:, SXW:SXW + 1], in0=colb[:, SX:SX + 1],
                                               scalar1=swb["w2"][:, :], scalar2=None,
                                               op0=ALU.mult))
            trAug = mkrow(rowAug[:, :], colb[:, SXW:SXW + 1], 1, [tsxw, trowB])
            tq_ev = None
            for hc in range(8):
                Vn(lambda e, h=hc: e.tensor_scalar(
                    out=h2c[:, :].bitcast(I32), in0=gbuf[:, 512 * h:512 * (h + 1)],
                    scalar1=colb[:, SX:SX + 1], scalar2=None, op0=ALU.mult),
                   waits=(tsx,) if hc == 0 else ())
                tqb = V(lambda e: e.tensor_copy(out=h2qb[:, :], in_=h2c[:, :].bitcast(I32)))
                w0 = [tqb] + ([tq_ev] if tq_ev else [prev_tx])
                tb = None
                for k in range(4):
                    fn = (lambda e, k=k: e.transpose(
                        out=ps_txq[:, P * k:P * (k + 1)],
                        in_=h2qb[:, P * k:P * (k + 1)], identity=identb[:, :]))
                    if k < 3:
                        Tn(fn, waits=tuple(w0) if k == 0 else ())
                    else:
                        tb = T(fn)
                tq_ev = V(lambda e, h=hc: e.tensor_copy(
                    out=h2qT[:, 512 * h:512 * (h + 1)], in_=ps_txq[:, 0:512]), waits=(tb,))
            prev_tx = tq_ev
            tstt = None
            for ec in range(2):
                w0 = [tq_ev] if ec == 0 else [tstt]
                tmm = None
                for ht in range(32):
                    fn = (lambda e, t=ht, x=ec: e.matmul(
                        out=ps_2[:, :], lhsT=h2qT[:, P * t:P * (t + 1)],
                        rhs=w2q[:, D * t + 512 * x:D * t + 512 * (x + 1)],
                        start=(t == 0), stop=False))
                    Tn(fn, waits=tuple(w0) if ht == 0 else ())
                tmm = T(lambda e, x=ec: e.matmul(
                    out=ps_2[:, :], lhsT=rowAug[:, :], rhs=cb2[0:1, 512 * x:512 * (x + 1)],
                    start=False, stop=True), waits=(trAug,))
                tstt = V(lambda e, g=g, x=ec: e.scalar_tensor_tensor(
                    out=scr1[:, 512 * x:512 * (x + 1)], in0=ps_2[:, :],
                    scalar=colb[:, DSC2:DSC2 + 1],
                    in1=streamT(g)[:, 512 * x:512 * (x + 1)],
                    op0=ALU.mult, op1=ALU.add), waits=(tmm,))
            ta3 = ln_stats(lambda c0, c1: scr1[:, c0:c1], D, w0=(tstt,))
            _, tnm3 = rstd_nmu(ta3)
            prev_w2 = ln_apply(streamT(g), scr1[:, :], tnm3)
        close_phase()
        g3.close()
        g_w12.close()
        g_l0c.close()
        tap("l0ffn")
        if n_phases <= 3:
            _emit_out(barrier, DG, C, cur, close_phase, stream, Y)
            return nc

        barrier()
        g4pre = contextlib.ExitStack()
        qTb = g4pre.enter_context(nc.sbuf_tensor("qTb", [64, L], BF16))
        wqs_b = g4pre.enter_context(nc.sbuf_tensor("wqs_b", [P, 2 * D], F32))
        wqi_b = scr1[:, :].bitcast(I32)
        wpart_b = bnb
        ktok = g4pre.enter_context(nc.sbuf_tensor("ktok", [P, NT * 64], F32))
        vtok = g4pre.enter_context(nc.sbuf_tensor("vtok", [P, NT * P], BF16))
        vtokR = vtok[:, :].rearrange("p (j h c) -> p j h c", h=64, c=2)
        EL0 = g4pre.enter_context(nc.sbuf_tensor("EL0", [P, L], BF16))
        EL1 = g4pre.enter_context(nc.sbuf_tensor("EL1", [P, L], BF16))
        qrow = g4pre.enter_context(nc.sbuf_tensor("qrow", [1, L], BF16))
        hn_tok = g4pre.enter_context(nc.sbuf_tensor("hn_tok", [P, NT * 64], BF16))
        rdb = g4pre.enter_context(nc.sbuf_tensor("rdb", [P, 64], F32))

        g4v = contextlib.ExitStack()
        vkf = g4v.enter_context(nc.sbuf_tensor("vkf", [P, L], F32))
        g4x = contextlib.ExitStack()
        X1T = g4x.enter_context(nc.sbuf_tensor("X1T", [P, 8 * L], BF16))
        X1Tr = X1T[:, :].rearrange("p (t i) -> p t i", i=L)
        wq1b = g4x.enter_context(nc.sbuf_tensor("wq1b", [P, 1536], BF16))
        wq1s = g4x.enter_context(nc.sbuf_tensor("wq1s", [P, 1536], F32))
        g4a = contextlib.ExitStack()
        ps_f0 = g4a.enter_context(nc.psum_tensor("ps_f0", [P, 512], F32))
        ps_f1 = g4a.enter_context(nc.psum_tensor("ps_f1", [P, 512], F32))
        ps_q4 = g4a.enter_context(nc.psum_tensor("ps_q4", [P, 1], F32))

        wq_g = WQ(WGT, D, "wg", None, wqs_b, wqi_b, wpart_b, ps_q4)
        wq_u = WQ(WUT, D, "wu", None, wqs_b, wqi_b, wpart_b, ps_q4)
        wq_d = WQ(WDT, HD, "wd", None, wqs_b, wqi_b, wpart_b, ps_q4)
        wq_list1 = [wq_g, wq_u, wq_d]

        def wq1_tick(n):
            for _ in range(n):
                if not wq_list1:
                    return
                if wq_list1[0].step1():
                    done = wq_list1.pop(0)
                    if wq_list1:
                        wq_list1[0].prev = dict(done.prev)

        ev_tok = {0: None, 1: None}
        for g in range(NT):
            for bb in range(2):
                bank = [ps_f0, ps_f1][bb]
                w0 = [] if ev_tok[bb] is None else [ev_tok[bb]]
                tb = None
                for k in range(4):
                    dt = 4 * bb + k
                    fn = (lambda e, g=g, d=dt, b=bank, k=k: e.transpose(
                        out=b[:, P * k:P * (k + 1)],
                        in_=stream[:, D * g + P * d:D * g + P * (d + 1)],
                        identity=identf[:, :]))
                    if k < 3:
                        Tn(fn, waits=tuple(w0) if k == 0 else ())
                    else:
                        tb = T(fn)
                ev_tok[bb] = V(lambda e, g=g, bb=bb, b=bank: e.tensor_copy(
                    out=X1Tr[:, 4 * bb:4 * bb + 4, P * g:P * (g + 1)],
                    in_=b[:, :].rearrange("p (t c) -> p t c", c=P)), waits=(tb,))
            wq1_tick(4)
        td = DS(wq1s[:, :].rearrange("p (t c) -> p t c", c=192),
                WQKV1T[:, :].rearrange("(t p) c -> p t c", p=P))
        tcst = V(lambda e: e.tensor_copy(out=wq1b[:, :], in_=wq1s[:, :]), waits=(td,))
        evq = {0: None, 1: None}
        for fc in range(2):
            fw = P if fc == 0 else 64
            for ic in range(4):
                bid = ic % 2
                bank = [ps_f0, ps_f1][bid]
                w0 = [tcst, ev_tok[0], ev_tok[1]] if evq[bid] is None else [evq[bid]]
                tmm = None
                for dt in range(8):
                    fn = (lambda e, f=fc, d=dt, i=ic, b=bank, fw=fw: e.matmul(
                        out=b[0:fw, :], lhsT=wq1b[:, 192 * d + P * f:192 * d + P * f + fw],
                        rhs=X1Tr[:, d, 512 * i:512 * (i + 1)],
                        start=(d == 0), stop=(d == 7)))
                    if dt < 7:
                        Tn(fn, waits=tuple(w0) if dt == 0 else ())
                    else:
                        tmm = T(fn)
                if fc == 0:
                    Vn(lambda e, i=ic, b=bank: e.tensor_copy(
                        out=qTb[0:64, 512 * i:512 * (i + 1)], in_=b[0:64, :]), waits=(tmm,))
                    evq[bid] = V(lambda e, i=ic, b=bank: e.tensor_copy(
                        out=vkf[64:P, 512 * i:512 * (i + 1)], in_=b[64:P, :]))
                else:
                    evq[bid] = V(lambda e, i=ic, b=bank: e.tensor_copy(
                        out=vkf[0:64, 512 * i:512 * (i + 1)], in_=b[0:64, :]), waits=(tmm,))
                wq1_tick(4)
        wq1_tick(300)
        close_phase()
        g4a.close()
        g4x.close()

        barrier()
        g4d = contextlib.ExitStack()
        ps_k = g4d.enter_context(nc.psum_tensor("ps_k", [P, 64], F32))
        ps_v = g4d.enter_context(nc.psum_tensor("ps_v", [P, 64], F32))
        Vn(lambda e: e.memset(vtokR[:, :, :, 1:2], 1.0))
        tk_ev = None
        tv_ev = None
        for jt in range(NT):
            w0 = [] if tk_ev is None else [tk_ev]
            tkk = T(lambda e, j=jt: e.transpose(out=ps_k[:, :],
                                                in_=vkf[64:P, P * j:P * (j + 1)],
                                                identity=identf[64:P, 64:P]),
                    waits=tuple(w0))
            tk_ev = V(lambda e, j=jt: e.tensor_scalar(
                out=ktok[:, 64 * j:64 * (j + 1)], in0=ps_k[:, :], scalar1=0.125,
                scalar2=None, op0=ALU.mult), waits=(tkk,))
            w0 = [] if tv_ev is None else [tv_ev]
            tvv = T(lambda e, j=jt: e.transpose(out=ps_v[:, :],
                                                in_=vkf[0:64, P * j:P * (j + 1)],
                                                identity=identf[0:64, 0:64]),
                    waits=tuple(w0))
            tv_ev = V(lambda e, j=jt: e.tensor_copy(
                out=vtokR[:, j, :, 0:1], in_=ps_v[:, :].rearrange("p (h c) -> p h c", c=1)),
                waits=(tvv,))
        close_phase()
        g4d.close()

        g4v.close()

        g5w = contextlib.ExitStack()
        wgq = g5w.enter_context(nc.sbuf_tensor("wgq", [P, 8 * HD], FP8))
        wuq = g5w.enter_context(nc.sbuf_tensor("wuq", [P, 8 * HD], FP8))
        wdq = g5w.enter_context(nc.sbuf_tensor("wdq", [P, 32 * D], FP8))
        wq_g.dst = wgq
        wq_u.dst = wuq
        wq_d.dst = wdq
        wq_list2 = [wq_g, wq_u, wq_d]

        def wq2_tick(n):
            for _ in range(n):
                if not wq_list2:
                    return
                if wq_list2[0].step2():
                    done = wq_list2.pop(0)
                    if wq_list2:
                        wq_list2[0].prev2 = dict(done.prev2)

        barrier()
        g4e = contextlib.ExitStack()
        ps_qb = g4e.enter_context(nc.psum_tensor("ps_qb", [P, 2 * D], F32))
        ps_avL = g4e.enter_context(nc.psum_tensor("ps_avL", [P, 2 * D], F32))
        ps_avR = ps_avL[:, :].rearrange("p (i h c) -> p i h c", h=64, c=2)
        Ebufs = [EL0, EL1]
        t_qrow = {}
        t_qbc = {}
        t_expdone = {}
        t_avpar = {}
        tlastmm = None
        for h in range(64):
            w0 = [t_qbc[h - 1]] if h >= 1 else []
            t_qrow[h] = DG(qrow[0:1, :], qTb[h:h + 1, :], waits=tuple(w0))
            w0 = [t_qrow[h]]
            if h >= 1:
                w0.append(t_expdone[h - 1])
            tqb = None
            for ic in range(4):
                fn = (lambda e, i=ic: e.matmul(
                    out=ps_qb[:, 512 * i:512 * (i + 1)], lhsT=ones_b[:, :],
                    rhs=qrow[0:1, 512 * i:512 * (i + 1)], start=True, stop=True))
                if ic < 3:
                    Tn(fn, waits=tuple(w0) if ic == 0 else ())
                else:
                    tqb = T(fn)
            t_qbc[h] = tqb
            tle = None
            for jt in range(NT):
                par = jt % 2
                Eb = Ebufs[par]
                i0 = P * jt
                w0 = [tqb]
                if par in t_avpar:
                    w0.append(t_avpar[par])
                insts = [(i0, D), (D, L)] if i0 < D else [(i0, L)]
                tlast = None
                for (ia, ib) in insts:
                    tlast = S(lambda e, h=h, Eb=Eb, j=jt, ia=ia, ib=ib: e.activation(
                        out=Eb[:, ia:ib], in_=ps_qb[:, ia:ib], func=AF.Exp,
                        scale=ktok[:, 64 * j + h:64 * j + h + 1]),
                        waits=tuple(w0) if (ia, ib) == insts[0] else ())
                tle = tlast
                tmsk = V(lambda e, Eb=Eb, j=jt: e.tensor_tensor(
                    out=Eb[:, P * j:P * (j + 1)], in0=Eb[:, P * j:P * (j + 1)],
                    in1=cmask[:, :], op=ALU.mult), waits=(tlast,))
                for it in range(jt, NT):
                    st = (h == 0 and jt == 0 and it % 4 == 0)
                    w1 = [tmsk] if it == jt else ()
                    fn = (lambda e, Eb=Eb, h=h, j=jt, t=it, st=st: e.matmul(
                        out=ps_avL[:, P * t + 2 * h:P * t + 2 * (h + 1)],
                        lhsT=Eb[:, P * t:P * (t + 1)],
                        rhs=vtokR[:, j, h, :],
                        start=st, stop=(h == 63 and j == t),
                        skip_group_check=(not st)))
                    if it == NT - 1:
                        tlastmm = T(fn, waits=tuple(w1))
                        t_avpar[par] = tlastmm
                    else:
                        Tn(fn, waits=tuple(w1))
            t_expdone[h] = tle
            wq2_tick(2)
        wq2_tick(300)
        t_avdone = {63: tlastmm}
        g4f = contextlib.ExitStack()
        hnT = g4f.enter_context(nc.sbuf_tensor("hnT", [64, L], BF16))
        wo1b = g4f.enter_context(nc.sbuf_tensor("wo1b", [64, D], BF16))
        wo1s = wqs_b[0:64, 0:D]
        thn = {}
        for it in range(NT):
            trd = V(lambda e, t=it: e.reciprocal(out=rdb[:, :], in_=ps_avR[:, t, :, 1:2]),
                    waits=(t_avdone[63],))
            thn[it] = V(lambda e, t=it: e.tensor_tensor(
                out=hn_tok[:, 64 * t:64 * (t + 1)],
                in0=ps_avR[:, t, :, 0], in1=rdb[:, :], op=ALU.mult), waits=(trd,))
        close_phase()
        g4e.close()

        barrier()
        g4g = contextlib.ExitStack()
        ps_ht = g4g.enter_context(nc.psum_tensor("ps_ht", [64, 512], BF16))
        ps_so = g4g.enter_context(nc.psum_tensor("ps_so", [P, D], F32))
        td = DS(wo1s, WOUT1T, waits=(("vS", C["vS"]),))
        tw1o = V(lambda e: e.tensor_copy(out=wo1b[:, :], in_=wo1s), waits=(td,))
        tx_ev = None
        for bb in range(4):
            w0 = [] if tx_ev is None else [tx_ev]
            tb = None
            for k in range(4):
                it = 4 * bb + k
                fn = (lambda e, t=it, k=k: e.transpose(
                    out=ps_ht[:, P * k:P * (k + 1)], in_=hn_tok[:, 64 * t:64 * (t + 1)],
                    identity=identb[:, :]))
                if k < 3:
                    Tn(fn, waits=tuple(w0) if k == 0 else ())
                else:
                    tb = T(fn)
            tx_ev = V(lambda e, bb=bb: e.tensor_copy(
                out=hnT[:, 512 * bb:512 * (bb + 1)], in_=ps_ht[:, :]), waits=(tb,))
        t_ln = None
        for g in range(NT):
            w0 = [tx_ev, tw1o] if g == 0 else [t_ln]
            tmm = None
            for ec in range(2):
                fn = (lambda e, g=g, x=ec: e.matmul(
                    out=ps_so[:, 512 * x:512 * (x + 1)], lhsT=hnT[:, P * g:P * (g + 1)],
                    rhs=wo1b[:, 512 * x:512 * (x + 1)], start=True, stop=True))
                if ec == 0:
                    Tn(fn, waits=tuple(w0))
                else:
                    tmm = T(fn)
            tadd = V(lambda e, g=g: e.tensor_tensor(out=scr1[:, :], in0=ps_so[:, :],
                                                    in1=streamT(g), op=ALU.add),
                     waits=(tmm,))
            ta = ln_stats(lambda c0, c1: scr1[:, c0:c1], D)
            _, tnm = rstd_nmu(ta)
            t_ln = ln_apply(streamT(g), scr1[:, :], tnm)
        close_phase()
        g4g.close()
        g4f.close()
        tap("l1attn")
        if n_phases <= 4:
            _emit_out(barrier, DG, C, cur, close_phase, stream, Y)
            return nc

        barrier()

        g5 = contextlib.ExitStack()
        gu_half = [EL0[:, :].bitcast(F32), EL1[:, :].bitcast(F32),
                   ktok[:, :], vtok[:, :].bitcast(F32)]

        def gu_c(hc):
            return gu_half[hc // 2][:, 512 * (hc % 2):512 * (hc % 2 + 1)]
        xq5 = g5.enter_context(nc.sbuf_tensor("xq5", [P, D], BF16))
        xqT5 = g5.enter_context(nc.sbuf_tensor("xqT5", [P, D], BF16))
        sil = hn_tok[:, :].bitcast(F32)
        guqb = g5.enter_context(nc.sbuf_tensor("guqb", [P, 512], BF16))
        guqT = wqs_b[:, :].bitcast(BF16)
        ps_g = g5.enter_context(nc.psum_tensor("ps_g", [P, 512], F32))
        ps_u = g5.enter_context(nc.psum_tensor("ps_u", [P, 512], F32))
        ps_d = g5.enter_context(nc.psum_tensor("ps_d", [P, 512], F32))
        ps_tx5 = g5.enter_context(nc.psum_tensor("ps_tx5", [P, D], BF16))

        prev_ln = None
        prev_tx = None
        prev_gu = None
        prev_tq = None
        st_toks = []
        for g in range(NT):
            w0 = [prev_ln] if prev_ln else []
            t1 = V(lambda e, g=g: e.reduce_max(out=colb[:, RM:RM + 1], in_=streamT(g),
                                               axis=AX.X, apply_absolute_value=True),
                   waits=tuple(w0))
            t2 = V(lambda e: e.tensor_scalar(out=colb[:, MCL:MCL + 1], in0=colb[:, RM:RM + 1],
                                             scalar1=1e-5, scalar2=None, op0=ALU.max),
                   waits=(t1,))
            t3 = V(lambda e: e.reciprocal(out=colb[:, RC:RC + 1], in_=colb[:, MCL:MCL + 1]),
                   waits=(t2,))
            tsx = V(lambda e: e.tensor_scalar(out=colb[:, SX:SX + 1], in0=colb[:, RC:RC + 1],
                                              scalar1=127.0, scalar2=None, op0=ALU.mult),
                    waits=(t3,))
            Vn(lambda e: e.tensor_scalar(out=colb[:, DSC:DSC + 1], in0=colb[:, MCL:MCL + 1],
                                         scalar1=1.0 / 127.0, scalar2=dscwb["wg"][:, :],
                                         op0=ALU.mult, op1=ALU.mult))
            Vn(lambda e: e.tensor_scalar(out=colb[:, DSC2:DSC2 + 1], in0=colb[:, MCL:MCL + 1],
                                         scalar1=1.0 / 127.0, scalar2=dscwb["wu"][:, :],
                                         op0=ALU.mult, op1=ALU.mult))
            Vn(lambda e, g=g: e.tensor_scalar(out=gu_half[0].bitcast(I32), in0=streamT(g),
                                              scalar1=colb[:, SX:SX + 1], scalar2=None,
                                              op0=ALU.mult), waits=(tsx,))
            txq = V(lambda e: e.tensor_copy(out=xq5[:, :], in_=gu_half[0].bitcast(I32)))
            w0 = [txq] + ([prev_tx] if prev_tx else [])
            tb = None
            for dt in range(8):
                fn = (lambda e, d=dt: e.transpose(out=ps_tx5[:, P * d:P * (d + 1)],
                                                  in_=xq5[:, P * d:P * (d + 1)],
                                                  identity=identb[:, :]))
                if dt < 7:
                    Tn(fn, waits=tuple(w0) if dt == 0 else ())
                else:
                    tb = T(fn)
            prev_tx = V(lambda e: e.tensor_copy(out=xqT5[:, :], in_=ps_tx5[:, :]), waits=(tb,))
            mxt = None
            for hc in range(8):
                w0 = [prev_tx]
                if prev_gu is not None:
                    w0.append(prev_gu)
                tg = None
                for dt in range(8):
                    fn = (lambda e, d=dt, h=hc: e.matmul(
                        out=ps_g[:, :], lhsT=xqT5[:, P * d:P * (d + 1)],
                        rhs=wgq[:, HD * d + 512 * h:HD * d + 512 * (h + 1)],
                        start=(d == 0), stop=(d == 7)))
                    if dt < 7:
                        Tn(fn, waits=tuple(w0) if dt == 0 else ())
                    else:
                        tg = T(fn)
                tu = None
                for dt in range(8):
                    fn = (lambda e, d=dt, h=hc: e.matmul(
                        out=ps_u[:, :], lhsT=xqT5[:, P * d:P * (d + 1)],
                        rhs=wuq[:, HD * d + 512 * h:HD * d + 512 * (h + 1)],
                        start=(d == 0), stop=(d == 7)))
                    if dt < 7:
                        Tn(fn)
                    else:
                        tu = T(fn)
                sw = [tg] + ([prev_gu] if prev_gu else [])
                tsil = S(lambda e: e.activation(out=sil, in_=ps_g[:, :], func=AF.Silu,
                                                scale=colb[:, DSC:DSC + 1]),
                         waits=tuple(sw))
                prev_gu = V(lambda e, h=hc: e.scalar_tensor_tensor(
                    out=gu_c(h), in0=ps_u[:, :],
                    scalar=colb[:, DSC2:DSC2 + 1], in1=sil,
                    op0=ALU.mult, op1=ALU.mult), waits=(tu, tsil))
                mxt = V(lambda e, h=hc: e.reduce_max(
                    out=mx8[:, h:h + 1], in_=gu_c(h), axis=AX.X,
                    apply_absolute_value=True))
            Vn(lambda e: e.reduce_max(out=colb[:, RM:RM + 1], in_=mx8[:, :], axis=AX.X),
               waits=(mxt,))
            Vn(lambda e: e.tensor_scalar(out=colb[:, MCL:MCL + 1], in0=colb[:, RM:RM + 1],
                                         scalar1=1e-5, scalar2=None, op0=ALU.max))
            Vn(lambda e: e.reciprocal(out=colb[:, RC:RC + 1], in_=colb[:, MCL:MCL + 1]))
            tsx = V(lambda e: e.tensor_scalar(out=colb[:, SX:SX + 1], in0=colb[:, RC:RC + 1],
                                              scalar1=127.0, scalar2=None, op0=ALU.mult))
            Vn(lambda e: e.tensor_scalar(out=colb[:, DSC:DSC + 1], in0=colb[:, MCL:MCL + 1],
                                         scalar1=1.0 / 127.0, scalar2=dscwb["wd"][:, :],
                                         op0=ALU.mult, op1=ALU.mult))
            tq_ev = None
            for hc in range(8):
                Vn(lambda e, h=hc: e.tensor_scalar(
                    out=sil.bitcast(I32), in0=gu_c(h),
                    scalar1=colb[:, SX:SX + 1], scalar2=None, op0=ALU.mult),
                   waits=(tsx,) if hc == 0 else ())
                tqb = V(lambda e: e.tensor_copy(out=guqb[:, :], in_=sil.bitcast(I32)))
                w0 = [tqb] + ([tq_ev] if tq_ev else [prev_tx])
                tb = None
                for k in range(4):
                    fn = (lambda e, k=k: e.transpose(
                        out=ps_tx5[:, P * k:P * (k + 1)], in_=guqb[:, P * k:P * (k + 1)],
                        identity=identb[:, :]))
                    if k < 3:
                        Tn(fn, waits=tuple(w0) if k == 0 else ())
                    else:
                        tb = T(fn)
                tq_ev = V(lambda e, h=hc: e.tensor_copy(
                    out=guqT[:, 512 * h:512 * (h + 1)], in_=ps_tx5[:, 0:512]), waits=(tb,))
            prev_tx = tq_ev
            tstt = None
            for ec in range(2):
                w0 = [tq_ev] if ec == 0 else [tstt]
                tmm = None
                for ht in range(32):
                    fn = (lambda e, t=ht, x=ec: e.matmul(
                        out=ps_d[:, :], lhsT=guqT[:, P * t:P * (t + 1)],
                        rhs=wdq[:, D * t + 512 * x:D * t + 512 * (x + 1)],
                        start=(t == 0), stop=(t == 31)))
                    if ht < 31:
                        Tn(fn, waits=tuple(w0) if ht == 0 else ())
                    else:
                        tmm = T(fn)
                tstt = V(lambda e, g=g, x=ec: e.scalar_tensor_tensor(
                    out=scr1[:, 512 * x:512 * (x + 1)], in0=ps_d[:, :],
                    scalar=colb[:, DSC:DSC + 1],
                    in1=streamT(g)[:, 512 * x:512 * (x + 1)],
                    op0=ALU.mult, op1=ALU.add), waits=(tmm,))
            ta = ln_stats(lambda c0, c1: scr1[:, c0:c1], D, w0=(tstt,))
            _, tnm = rstd_nmu(ta)
            prev_ln = ln_apply(streamT(g), scr1[:, :], tnm)
            st_toks.append(DG(Y[P * g:P * (g + 1), :], streamT(g), waits=(prev_ln,)))
        q("gpsimd", None, None, (("qG", C["qG"]),))
        close_phase()
        g5.close()
        g5w.close()
        g4pre.close()
    return nc


def _emit_out(barrier, DG, C, cur, close_phase, stream, Y):
    barrier()
    for g in range(NT):
        DG(Y[P * g:P * (g + 1), :], stream[:, D * g:D * (g + 1)])
    cur["gpsimd"].append(((("qG", C["qG"]),), None, None))
    close_phase()


_BF = ml_dtypes.bfloat16


def _prep(inputs):
    f = np.float32
    c = np.ascontiguousarray
    shared = {
        "wqkvT": c(inputs["L0_attn_w_qkv"].T.astype(f)),
        "bqkvT": c(inputs["L0_attn_b_qkv"].astype(f).reshape(24, P).T),
        "woT": c(inputs["L0_attn_w_o"].T.astype(f)),
        "borow": c(inputs["L0_attn_b_o"].astype(f).reshape(1, D)),
        "bvrow": c(inputs["L0_attn_b_qkv"][2 * D:3 * D].astype(f).reshape(1, D)),
        "ln1gb": c(np.stack([inputs["L0_ffn_ln1_g"], inputs["L0_ffn_ln1_b"]]).astype(f)),
        "ln2gb": c(np.stack([inputs["L0_ffn_ln2_g"], inputs["L0_ffn_ln2_b"]]).astype(f)),
        "b1row": c(inputs["L0_ffn_b1"].astype(f).reshape(1, HD)),
        "b2row": c(inputs["L0_ffn_b2"].astype(f).reshape(1, D)),
        "w1T": c(inputs["L0_ffn_w1"].T.astype(f)),
        "w2T": c(inputs["L0_ffn_w2"].T.astype(f)),
        "wqkv1T": c(inputs["L1_attn_w_qkv"].T.astype(f)),
        "wout1T": c(inputs["L1_attn_w_out"].T.astype(f)),
        "wgT": c(inputs["L1_ffn_w_gate"].T.astype(f)),
        "wuT": c(inputs["L1_ffn_w_up"].T.astype(f)),
        "wdT": c(inputs["L1_ffn_w_down"].T.astype(f)),
        "identf": np.eye(P, dtype=f),
        "identb": np.eye(P).astype(_BF),
        "cmask": (np.arange(P)[None, :] >= np.arange(P)[:, None]).astype(_BF),
    }
    xs = np.asarray(inputs["L0_attn_x"], dtype=f)
    return shared, xs


def kernel(**inputs):
    shared, xs = _prep(inputs)
    nc = build()
    in_maps = [dict(shared, x=np.ascontiguousarray(xs[b])) for b in range(8)]
    res = run_bass_kernel_spmd(nc, in_maps, core_ids=list(range(8)))
    out = np.stack([r["y"] for r in res.results], axis=0)
    return out.astype(np.float32)
